# version 13
# baseline (speedup 1.0000x reference)
import numpy as np
import ml_dtypes
import concourse.bass as bass
import concourse.mybir as mybir
from concourse.ap import AP
from concourse.bass_utils import run_bass_kernel_spmd

F32 = mybir.dt.float32
BF16 = mybir.dt.bfloat16
U16 = mybir.dt.uint16
I32 = mybir.dt.int32
ALU = mybir.AluOpType
AF = mybir.ActivationFunctionType
AXX = mybir.AxisListType.X
f32 = np.float32

B, T, D, H, DH = 4, 2048, 2048, 32, 64
WIN = 1408
QN = 1024
NPAD = 384
NWC = WIN // 128
NQC = QN // 128
NDC = D // 128
MLP = 16384
NSUP = 8
NEG = -1.0e30
GELU_A = f32(0.7978845608028654 * 0.044715)
GELU_B = f32(0.7978845608028654)
RSQRT_MAGIC = 0x5F3759DF
LN1P_C = [1.0, -0.49999502, 0.3331927, -0.24844374, 0.19111443,
          -0.13674787, 0.07836182, -0.02958858, 0.00525347]


def _build(debug=False, detect_races=True, phases=9, p1n=99, p1tr=True, p1ln=True, p3o=True, p3n=99):
    nc = bass.Bass("TRN2", target_bir_lowering=False, debug=False,
                   detect_race_conditions=detect_races)

    xw = nc.dram_tensor("xw", [WIN, D], F32, kind="ExternalInput").ap()
    km_in = nc.dram_tensor("km", [128, NWC], F32, kind="ExternalInput").ap()
    wqT = nc.dram_tensor("wqT", [D, D], F32, kind="ExternalInput").ap()
    wkT = nc.dram_tensor("wkT", [D, D], F32, kind="ExternalInput").ap()
    wvT = nc.dram_tensor("wvT", [D, D], F32, kind="ExternalInput").ap()
    woT = nc.dram_tensor("woT", [D, D], F32, kind="ExternalInput").ap()
    wfT_in = nc.dram_tensor("wfT", [D, H], F32, kind="ExternalInput").ap()
    bfr_in = nc.dram_tensor("bfr", [1, H], F32, kind="ExternalInput").ap()
    gpk = nc.dram_tensor("gpk", [MLP, D // 2], U16, kind="ExternalInput").ap()
    upk = nc.dram_tensor("upk", [MLP, D // 2], U16, kind="ExternalInput").ap()
    dpk = nc.dram_tensor("dpk", [D, MLP // 2], U16, kind="ExternalInput").ap()
    gsc = nc.dram_tensor("gsc", [D // 2, MLP], F32, kind="ExternalInput").ap()
    usc = nc.dram_tensor("usc", [D // 2, MLP], F32, kind="ExternalInput").ap()
    dsc = nc.dram_tensor("dsc", [MLP // 2, D], F32, kind="ExternalInput").ap()
    out = nc.dram_tensor("out", [QN, D], F32, kind="ExternalOutput").ap()

    out2d = nc.dram_tensor("out2d", [D, QN], F32, kind="Internal").ap()

    dbg = {}
    if debug:
        dbg["xt1"] = nc.dram_tensor("dbg_xt1", [128, NDC * WIN], BF16, kind="ExternalOutput").ap()
        dbg["logf"] = nc.dram_tensor("dbg_logf", [128, NWC * H], F32, kind="ExternalOutput").ap()
        dbg["ccol"] = nc.dram_tensor("dbg_ccol", [128, NWC * H], F32, kind="ExternalOutput").ap()
        dbg["att"] = nc.dram_tensor("dbg_att", [128, NDC * QN], BF16, kind="ExternalOutput").ap()
        dbg["x2"] = nc.dram_tensor("dbg_x2", [QN, D], F32, kind="ExternalOutput").ap()
        dbg["xt2"] = nc.dram_tensor("dbg_xt2", [128, NDC * QN], BF16, kind="ExternalOutput").ap()
        dbg["inter"] = nc.dram_tensor("dbg_inter", [128, 4 * QN], BF16, kind="ExternalOutput").ap()
        dbg["gw"] = nc.dram_tensor("dbg_gw", [128, 16 * 512], BF16, kind="ExternalOutput").ap()
        dbg["out2d"] = nc.dram_tensor("dbg_out2d", [2048, 1024], F32, kind="ExternalOutput").ap()
        dbg["dwnw"] = nc.dram_tensor("dbg_dwnw", [128, 4096], BF16, kind="ExternalOutput").ap()
        dbg["dstg0"] = nc.dram_tensor("dbg_dstg0", [128, 512], F32, kind="ExternalOutput").ap()
        dbg["dstg1"] = nc.dram_tensor("dbg_dstg1", [128, 512], F32, kind="ExternalOutput").ap()
        dbg["dpkb"] = nc.dram_tensor("dbg_dpkb", [128, 256], U16, kind="ExternalOutput").ap()
        dbg["dscb"] = nc.dram_tensor("dbg_dscb", [128, 256], F32, kind="ExternalOutput").ap()
        dbg["xln2c"] = nc.dram_tensor("dbg_xln2c", [128, D], BF16, kind="ExternalOutput").ap()
        dbg["xw2c"] = nc.dram_tensor("dbg_xw2c", [128, D], F32, kind="ExternalOutput").ap()
        dbg["stat2c"] = nc.dram_tensor("dbg_stat2c", [128, 16], F32, kind="ExternalOutput").ap()

    identf = nc.inline_tensor(np.eye(128, dtype=np.float32), "identf")
    identb = nc.inline_tensor(np.eye(128, dtype=ml_dtypes.bfloat16), "identb")

    from contextlib import ExitStack
    with ExitStack() as st_:
        e = st_.enter_context

        STAT = e(nc.sbuf_tensor("STAT", [128, 16], F32))
        KM = e(nc.sbuf_tensor("KM", [128, 16], F32))
        LOGF = e(nc.sbuf_tensor("LOGF", [128, NWC * H], F32))
        LTMP = e(nc.sbuf_tensor("LTMP", [128, H], F32))
        LTM2 = e(nc.sbuf_tensor("LTM2", [128, H], F32))
        UALL = e(nc.sbuf_tensor("UALL", [128, NWC * H], F32))
        HORN = e(nc.sbuf_tensor("HORN", [128, NWC * H], F32))
        CCOL = e(nc.sbuf_tensor("CCOL", [128, NWC * H], F32))
        NEGC = e(nc.sbuf_tensor("NEGC", [128, NWC * H], F32))
        TRIF = e(nc.sbuf_tensor("TRIF", [128, 128], F32))
        ONESF = e(nc.sbuf_tensor("ONESF", [128, 128], F32))
        ONESRF = e(nc.sbuf_tensor("ONESRF", [1, 128], F32))
        ONESRB = e(nc.sbuf_tensor("ONESRB", [1, 128], BF16))
        IDENTF = e(nc.sbuf_tensor("IDENTF", [128, 128], F32))
        IDENTB = e(nc.sbuf_tensor("IDENTB", [128, 128], BF16))
        WFT = e(nc.sbuf_tensor("WFT", [128, NDC * H], BF16))
        BFRB = e(nc.sbuf_tensor("BFRB", [1, H], BF16))
        WRM = e(nc.sbuf_tensor("WRM", [128, 64], F32))
        ONESR64 = e(nc.sbuf_tensor("ONESR64", [65, 128], F32))

        AR = e(nc.sbuf_tensor("AR", [128, 49152], F32))

        used = 229344 - nc.sbuf_bytes_remaining
        assert used <= 229344, f"SBUF overflow: {used}"

        def at(name, shape, dtype, off):
            esz = {F32: 4, BF16: 2, U16: 2, I32: 4}[dtype]
            nbytes = shape[1] * esz
            assert off % 4 == 0 and off + nbytes <= 196608, f"{name} arena {off}+{nbytes}"
            v = AR.ap()[:, off // 4:(off + nbytes) // 4]
            if dtype != F32:
                v = v.bitcast(dtype)
            if shape[0] != 128:
                v = v[0:shape[0], :]
            return v

        ATT = at("ATT", [128, NDC * QN], BF16, 0)
        XT2 = at("XT2", [128, NDC * QN], BF16, 32768)
        XT1 = at("XT1", [128, NDC * WIN], BF16, 32768)
        WPT = [at(f"WPT{i}", [128, NDC * 512], BF16, 77824 + i * 16384) for i in range(2)]
        Qg = at("Qg", [128, 4 * QN], BF16, 110592)
        Kg = at("Kg", [128, 4 * WIN], BF16, 118784)
        Vg = at("Vg", [128, NWC * 520], BF16, 130048)
        PT = [at(f"PT{i}", [128, 512], BF16, 141504 + i * 1024) for i in range(2)]
        RRB = at("RRB", [65, 512], F32, 143552)
        RBC = at("RBC", [64, 512], F32, 145600)
        OSC = [at(f"OSC{i}", [64, 512], BF16, 186368 + i * 1024) for i in range(2)]
        SELS = at("SELS", [32, 32 * 128], BF16, 148672)
        CROWB = at("CROWB", [32, QN], BF16, 156864)
        OTB = at("OTB", [128, NDC * QN], BF16, 65536)
        WOTB = [at(f"WOTB{i}", [128, NDC * 128], BF16, 98304 + i * 4096) for i in range(2)]
        XW = [at(f"XW{i}", [128, D], F32, 106496 + i * 8192) for i in range(2)]
        XLN = at("XLN", [128, D], BF16, 122880)
        SQS = at("SQS", [128, D], F32, 126976)
        X2S = at("X2S", [128, D], F32, 135168)
        OUTS = at("OUTS", [128, D], F32, 143360)
        INTER = at("INTER", [128, 16 * QN], BF16, 0)
        GW = [at(f"GW{i}", [128, 16 * 512], BF16, 65536 + i * 16384) for i in range(2)]
        UW = [at(f"UW{i}", [128, 16 * 512], BF16, 98304 + i * 16384) for i in range(2)]
        DWNW = [at(f"DWNW{i}", [128, 16 * 256], BF16, 131072 + i * 8192) for i in range(2)]
        SQT = at("SQT", [128, 512], F32, 147456)
        GF = at("GF", [128, 512], F32, 149504)
        U2T = at("U2T", [128, 512], F32, 151552)
        T1T = at("T1T", [128, 512], F32, 153600)
        YT = at("YT", [128, 512], F32, 155648)
        THT = at("THT", [128, 512], F32, 157696)
        QT = at("QT", [128, 512], F32, 159744)
        DSTG = [at(f"DSTG{i}", [128, 512], F32, 161792 + i * 2048) for i in range(2)]
        PKB = [at(f"PKB{i}", [128, 512], U16, 165888 + i * 1024) for i in range(4)]
        SCB = [at(f"SCB{i}", [128, 512], F32, 169984 + i * 2048) for i in range(4)]
        DF0B = at("DF0B", [128, 1024], U16, 178176)
        DSGN = at("DSGN", [128, 1024], U16, 180224)
        DF0 = at("DF0", [128, 1024], U16, 182272)
        DF1 = at("DF1", [128, 1024], U16, 184320)

        class Ctr:
            __slots__ = ("h", "n")
            def __init__(self, h):
                self.h = h
                self.n = 0
            def inc(self, inst, k):
                inst.then_inc(self.h, k)
                self.n += k
                return self.n

        qs = Ctr(e(nc.semaphore("qs")))
        qg = Ctr(e(nc.semaphore("qg")))
        sst = Ctr(e(nc.semaphore("sst")))
        spe = Ctr(e(nc.semaphore("spe")))
        sdv = Ctr(e(nc.semaphore("sdv")))
        sac = Ctr(e(nc.semaphore("sac")))
        spo = Ctr(e(nc.semaphore("spo")))
        block = e(nc.Block())
        emit_on = [True]
        def _mk(reg):
            return lambda f: (reg(f) if emit_on[0] else None)
        SY, VE, SC, TE, GP = (_mk(block.sync), _mk(block.vector), _mk(block.scalar),
                              _mk(block.tensor), _mk(block.gpsimd))

        @SY
        def _(q):
            q.dma_start(out=IDENTF[:], in_=identf.ap()).then_inc(qs.h, 16)
            q.dma_start(out=IDENTB[:], in_=identb.ap()).then_inc(qs.h, 16)
            q.dma_start(out=KM[:, 0:NWC], in_=km_in).then_inc(qs.h, 16)
        qs.n += 48
        t_km = qs.n

        @GP
        def _(g):
            g.dma_start(out=BFRB[:], in_=bfr_in).then_inc(qg.h, 16)
            g.dma_start(out=WFT[:], in_=wfT_in.rearrange("(c p) h -> p c h", p=128)).then_inc(qg.h, 16)
        qg.n += 32
        t_wf = qg.n

        @VE
        def _(v):
            v.memset(WRM[:], 1.0)
            v.memset(ONESF[:], 1.0)
            v.memset(ONESRF[:], 1.0)
            v.memset(ONESRB[:], 1.0)
            v.memset(ONESR64[64:65, :], 1.0)
            v.memset(TRIF[:], 1.0)
            v.memset(SELS[:], 1.0)
            inst = v.memset(STAT[:], 0.0)
            sdv.inc(inst, 1)
        t_const = sdv.n

        @GP
        def _(g):
            g.wait_ge(sdv.h, t_const)
            g.affine_select(out=TRIF[:], in_=TRIF[:], pattern=[[1, 128]],
                            compare_op=ALU.is_ge, fill=0.0, base=0, channel_multiplier=-1)
            for h in range(H):
                inst = g.affine_select(out=SELS[:, h * 128:(h + 1) * 128],
                                       in_=SELS[:, h * 128:(h + 1) * 128],
                                       pattern=[[0, 128]], compare_op=ALU.is_equal,
                                       fill=0.0, base=-h, channel_multiplier=1)
            spo.inc(inst, 1)
        t_masks = spo.n

        @SC
        def _(s):
            s.wait_ge(sdv.h, t_const)
            s.activation(out=WRM[:], in_=WRM[:], func=AF.Exp, bias=f32(0.0), scale=f32(0.1))
            s.activation(out=WRM[:], in_=WRM[:], func=AF.Sqrt, bias=f32(1.0), scale=f32(1.0))
            s.activation(out=WRM[:], in_=WRM[:], func=AF.Tanh, bias=f32(0.0), scale=f32(1.0))
            s.activation(out=WRM[:], in_=WRM[:], func=AF.Relu, bias=f32(1.0), scale=f32(-1.0))
            s.activation(out=WRM[:], in_=WRM[:], func=AF.Square, bias=f32(0.0), scale=f32(0.1))
            inst = s.activation(out=WRM[:], in_=WRM[:], func=AF.Exp, bias=f32(0.0), scale=f32(0.1))
            sac.inc(inst, 1)

        PS0 = e(nc.psum_tensor("PS0", [128, 512], F32))
        PS1 = e(nc.psum_tensor("PS1", [128, 512], F32))
        PS2 = e(nc.psum_tensor("PS2", [128, 512], F32))
        PS3 = e(nc.psum_tensor("PS3", [128, 512], F32))
        PS4 = e(nc.psum_tensor("PS4", [128, 512], F32))
        PS5 = e(nc.psum_tensor("PS5", [128, 512], F32))
        PS6 = e(nc.psum_tensor("PS6", [128, 512], F32))
        PSS = e(nc.psum_tensor("PSS", [128, 64], F32))
        PBT = [PS4.ap().bitcast(BF16)[:, 0:128], PS5.ap().bitcast(BF16)[:, 0:128]]
        PFT = [PS2.ap()[:, 0:128], PS3.ap()[:, 0:128]]

        def ln_apply(v, src):
            return v.tensor_scalar(out=XLN[:], in0=src[:], scalar1=STAT[:, 2:3],
                                   scalar2=STAT[:, 5:6], op0=ALU.subtract, op1=ALU.mult)

        def emit_ln_chain(stats_waits, apply_waits, src):
            @VE
            def _(v):
                for h, n in stats_waits:
                    v.wait_ge(h, n)
                v.reduce_sum(STAT[:, 0:1], src[:], axis=AXX)
                v.reduce_sum(STAT[:, 1:2], SQS[:], axis=AXX)
                v.tensor_single_scalar(out=STAT[:, 2:3], in_=STAT[:, 0:1], scalar=1.0 / D, op=ALU.mult)
                inst = v.tensor_single_scalar(out=STAT[:, 3:4], in_=STAT[:, 1:2], scalar=1.0 / D, op=ALU.mult)
                sdv.inc(inst, 1)
            t_st = sdv.n
            @SC
            def _(s, t_st=t_st):
                s.wait_ge(sdv.h, t_st)
                inst = s.activation(out=STAT[:, 4:5], in_=STAT[:, 2:3], func=AF.Square,
                                    bias=f32(0.0), scale=f32(1.0))
                sac.inc(inst, 1)
            t_msq = sac.n
            @VE
            def _(v, t_msq=t_msq):
                v.wait_ge(sac.h, t_msq)
                inst = v.tensor_tensor(out=STAT[:, 4:5], in0=STAT[:, 3:4], in1=STAT[:, 4:5],
                                       op=ALU.subtract)
                sdv.inc(inst, 1)
            t_var = sdv.n
            @SC
            def _(s, t_var=t_var):
                s.wait_ge(sdv.h, t_var)
                inst = s.activation(out=STAT[:, 6:7], in_=STAT[:, 4:5], func=AF.Sqrt,
                                    bias=f32(1e-5), scale=f32(1.0))
                sac.inc(inst, 1)
            t_sqv = sac.n
            @VE
            def _(v, t_sqv=t_sqv):
                v.wait_ge(sac.h, t_sqv)
                inst = v.reciprocal(STAT[:, 5:6], STAT[:, 6:7])
                sdv.inc(inst, 1)
                v.wait_ge(sdv.h, sdv.n)
                for h, n in apply_waits:
                    v.wait_ge(h, n)
                inst = ln_apply(v, src)
                sdv.inc(inst, 1)
            return sdv.n

        xw_t = xw.rearrange("(c p) d -> c p d", p=128)
        xw_free = [0, 0]
        xt1_tok = [0] * NWC
        tr_done = [0] * NWC

        for wc in range(min(NWC, p1n)):
            buf = wc % 2
            @SY
            def _(q, wc=wc, buf=buf):
                if wc >= 2:
                    q.wait_ge(sdv.h, xw_free[buf])
                q.dma_start(out=XW[buf][:], in_=xw_t[wc]).then_inc(qs.h, 16)
            qs.n += 16
            t_x = qs.n

            @SC
            def _(s, wc=wc, buf=buf, t_x=t_x):
                s.wait_ge(qs.h, t_x)
                if wc >= 1:
                    s.wait_ge(sdv.h, xt1_tok[wc - 1])
                inst = s.activation(out=SQS[:], in_=XW[buf][:], func=AF.Square,
                                    bias=f32(0.0), scale=f32(1.0))
                sac.inc(inst, 1)
            t_sq = sac.n

            t_xln = emit_ln_chain(
                stats_waits=[(qs.h, t_x), (sac.h, t_sq)],
                apply_waits=([(spe.h, tr_done[wc - 1])] if wc >= 1 else []),
                src=XW[buf])
            xw_free[buf] = t_xln

            for ck in range(NDC if p1tr else 0):
                @TE
                def _(t, wc=wc, ck=ck, t_xln=t_xln):
                    t.wait_ge(sdv.h, t_xln)
                    if ck >= 2:
                        t.wait_ge(sdv.h, sdv.n - 1)
                    inst = t.transpose(PBT[ck % 2], XLN[:, ck * 128:(ck + 1) * 128], IDENTB[:])
                    spe.inc(inst, 1)
                t_tr = spe.n
                @VE
                def _(v, wc=wc, ck=ck, t_tr=t_tr):
                    v.wait_ge(spe.h, t_tr)
                    inst = v.tensor_copy(XT1[:, ck * WIN + wc * 128: ck * WIN + wc * 128 + 128],
                                         PBT[ck % 2])
                    sdv.inc(inst, 1)
            xt1_tok[wc] = sdv.n
            tr_done[wc] = spe.n
            if debug and wc == 1:
                @SC
                def _(s, buf=buf, t_xln=t_xln):
                    s.wait_ge(sdv.h, t_xln)
                    s.dma_start(out=dbg["xln2c"], in_=XLN[:]).then_inc(sst.h, 16)
                    s.dma_start(out=dbg["xw2c"], in_=XW[buf][:]).then_inc(sst.h, 16)
                    s.dma_start(out=dbg["stat2c"], in_=STAT[:]).then_inc(sst.h, 16)
                sst.n += 48
        t_allxt1 = xt1_tok[NWC - 1]

        if debug:
            @SC
            def _(s):
                s.wait_ge(sdv.h, t_allxt1)
                s.dma_start(out=dbg["xt1"], in_=XT1[:]).then_inc(sst.h, 16)
            sst.n += 16

        emit_on[0] = phases >= 2
        t_pss_free = sdv.n
        for wc in range(NWC):
            @TE
            def _(t, wc=wc, t_pss_free=t_pss_free):
                t.wait_ge(sdv.h, xt1_tok[wc])
                t.wait_ge(qg.h, t_wf)
                t.wait_ge(sdv.h, t_pss_free)
                for ck in range(NDC):
                    t.matmul(PSS[:, 0:H], XT1[:, ck * WIN + wc * 128: ck * WIN + wc * 128 + 128],
                             WFT[:, ck * H:(ck + 1) * H], start=(ck == 0), stop=False)
                inst = t.matmul(PSS[:, 0:H], ONESRB[0:1, :], BFRB[0:1, :], start=False, stop=True)
                spe.inc(inst, 1)
            t_lf = spe.n

            @VE
            def _(v, wc=wc, t_lf=t_lf):
                v.wait_ge(spe.h, t_lf)
                inst = v.tensor_single_scalar(out=LOGF[:, wc * H:(wc + 1) * H], in_=PSS[:, 0:H],
                                              scalar=0.0, op=ALU.min)
                sdv.inc(inst, 1)

            @SC
            def _(s, wc=wc, t_lf=t_lf):
                s.wait_ge(spe.h, t_lf)
                s.activation(out=LTMP[:], in_=PSS[:, 0:H], func=AF.Exp, bias=f32(0.0), scale=f32(1.0))
                inst = s.activation(out=LTM2[:], in_=PSS[:, 0:H], func=AF.Exp, bias=f32(0.0), scale=f32(-1.0))
                sac.inc(inst, 1)
            t_ee = sac.n

            @VE
            def _(v, wc=wc, t_ee=t_ee):
                v.wait_ge(sac.h, t_ee)
                inst = v.tensor_tensor(out=UALL[:, wc * H:(wc + 1) * H],
                                       in0=LTMP[:], in1=LTM2[:], op=ALU.min)
                sdv.inc(inst, 1)
            t_pss_free = sdv.n

        t_u_all = sdv.n
        @VE
        def _(v):
            v.wait_ge(sdv.h, t_u_all)
            v.tensor_scalar(out=HORN[:], in0=UALL[:], scalar1=LN1P_C[8],
                            scalar2=LN1P_C[7], op0=ALU.mult, op1=ALU.add)
            for k in range(6, -1, -1):
                v.tensor_tensor(out=HORN[:], in0=HORN[:], in1=UALL[:], op=ALU.mult)
                v.tensor_single_scalar(out=HORN[:], in_=HORN[:], scalar=LN1P_C[k], op=ALU.add)
            v.tensor_tensor(out=HORN[:], in0=HORN[:], in1=UALL[:], op=ALU.mult)
            inst = v.tensor_tensor(out=LOGF[:], in0=LOGF[:], in1=HORN[:], op=ALU.subtract)
            sdv.inc(inst, 1)
        t_logf = sdv.n

        t_pssr = t_logf
        for wc in range(NWC):
            @TE
            def _(t, wc=wc, t_pssr=t_pssr):
                t.wait_ge(sdv.h, t_logf)
                t.wait_ge(sdv.h, t_pssr)
                for j in range(wc + 1):
                    lhsT = ONESF if j < wc else TRIF
                    inst = t.matmul(PSS[:, 0:H], lhsT[:],
                                    LOGF[:, j * H:(j + 1) * H], start=(j == 0), stop=(j == wc))
                spe.inc(inst, 1)
            t_c = spe.n

            @VE
            def _(v, wc=wc, t_c=t_c):
                v.wait_ge(spe.h, t_c)
                v.wait_ge(spo.h, t_masks)
                v.wait_ge(qs.h, t_km)
                v.tensor_copy(CCOL[:, wc * H:(wc + 1) * H], PSS[:, 0:H])
                inst = v.tensor_scalar(out=NEGC[:, wc * H:(wc + 1) * H],
                                       in0=PSS[:, 0:H],
                                       scalar1=KM[:, wc:wc + 1], scalar2=-1.0,
                                       op0=ALU.subtract, op1=ALU.mult)
                sdv.inc(inst, 1)
            t_pssr = sdv.n
        t_negc = sdv.n

        for qc in range(NQC):
            @TE
            def _(t, qc=qc):
                t.wait_ge(sdv.h, t_negc)
                if qc >= 2:
                    t.wait_ge(sdv.h, sdv.n - 1)
                inst = t.transpose(PFT[qc % 2][0:32, :], CCOL[:, (3 + qc) * H:(4 + qc) * H],
                                   IDENTF[:])
                spe.inc(inst, 1)
            t_tr = spe.n
            @VE
            def _(v, qc=qc, t_tr=t_tr):
                v.wait_ge(spe.h, t_tr)
                inst = v.tensor_copy(CROWB[0:32, qc * 128:(qc + 1) * 128], PFT[qc % 2][0:32, :])
                sdv.inc(inst, 1)
        t_crow = sdv.n

        emit_on[0] = phases >= 3
        KTIL = [(0, 512), (512, 512), (1024, 384)]
        wpt_free = [0, 0]

        @VE
        def _(v):
            for wc in range(NWC):
                for lh in range(8):
                    v.memset(Vg[:, wc * 520 + lh * 65 + 64: wc * 520 + lh * 65 + 65], 1.0)
            inst = v.memset(STAT[:, 15:16], 0.0)
            sdv.inc(inst, 1)
        t_vones = sdv.n

        psA_free = [0, 0, 0]
        oscr_qg = [0, 0]
        ps01_free = [0, 0]
        for gi in range(4):
            @GP
            def _(g, gi=gi):
                g.wait_ge(sdv.h, wpt_free[0])
                inst = g.dma_start(
                    out=WPT[0].rearrange("p (c n) -> p c n", c=NDC),
                    in_=wqT.rearrange("(c p) n -> p c n", p=128)[:, :, gi * 512:(gi + 1) * 512])
                inst.then_inc(qg.h, 16)
            qg.n += 16
            tok_q = qg.n
            for k in range(8):
                pr, ti = k // 2, k % 2
                ps = PS4 if k % 2 == 0 else PS5
                @TE
                def _(t, gi=gi, pr=pr, ti=ti, ps=ps, k=k, tok_q=tok_q, sg=sdv.n):
                    t.wait_ge(qg.h, tok_q)
                    t.wait_ge(sdv.h, t_allxt1)
                    if k >= 2:
                        t.wait_ge(sdv.h, sdv.n - 1)
                    else:
                        t.wait_ge(sdv.h, sg)
                    for ck in range(NDC):
                        inst = t.matmul(
                            ps[:], WPT[0][:, ck * 512 + pr * 128: ck * 512 + (pr + 1) * 128],
                            XT1[:, ck * WIN + NPAD + ti * 512: ck * WIN + NPAD + (ti + 1) * 512],
                            start=(ck == 0), stop=(ck == NDC - 1))
                    spe.inc(inst, 1)
                t_mm = spe.n
                @VE
                def _(v, pr=pr, ti=ti, ps=ps, t_mm=t_mm):
                    v.wait_ge(spe.h, t_mm)
                    inst = v.tensor_single_scalar(
                        out=Qg[:, pr * QN + ti * 512: pr * QN + (ti + 1) * 512],
                        in_=ps[:], scalar=0.125, op=ALU.mult)
                    sdv.inc(inst, 1)
            wpt_free[0] = sdv.n

            @GP
            def _(g, gi=gi):
                g.wait_ge(sdv.h, wpt_free[1])
                if gi == 0:
                    g.wait_ge(sdv.h, t_allxt1)
                inst = g.dma_start(
                    out=WPT[1].rearrange("p (c n) -> p c n", c=NDC),
                    in_=wkT.rearrange("(c p) n -> p c n", p=128)[:, :, gi * 512:(gi + 1) * 512])
                inst.then_inc(qg.h, 16)
            qg.n += 16
            tok_k = qg.n
            k = 0
            for pr in range(4):
                for (w0, wl) in KTIL:
                    ps = PS4 if k % 2 == 0 else PS5
                    @TE
                    def _(t, pr=pr, w0=w0, wl=wl, ps=ps, k=k, tok_k=tok_k, sg=sdv.n):
                        t.wait_ge(qg.h, tok_k)
                        if k >= 2:
                            t.wait_ge(sdv.h, sdv.n - 1)
                        else:
                            t.wait_ge(sdv.h, sg)
                        for ck in range(NDC):
                            inst = t.matmul(
                                ps[:, 0:wl],
                                WPT[1][:, ck * 512 + pr * 128: ck * 512 + (pr + 1) * 128],
                                XT1[:, ck * WIN + w0: ck * WIN + w0 + wl],
                                start=(ck == 0), stop=(ck == NDC - 1))
                        spe.inc(inst, 1)
                    t_mm = spe.n
                    @VE
                    def _(v, pr=pr, w0=w0, wl=wl, ps=ps, t_mm=t_mm):
                        v.wait_ge(spe.h, t_mm)
                        inst = v.tensor_copy(Kg[:, pr * WIN + w0: pr * WIN + w0 + wl], ps[:, 0:wl])
                        sdv.inc(inst, 1)
                    k += 1
            wpt_free[1] = sdv.n

            @GP
            def _(g, gi=gi):
                g.wait_ge(sdv.h, wpt_free[0])
                inst = g.dma_start(
                    out=WPT[0].rearrange("p (c n) -> p c n", c=NDC),
                    in_=wvT.rearrange("(c p) n -> p c n", p=128)[:, :, gi * 512:(gi + 1) * 512])
                inst.then_inc(qg.h, 16)
            qg.n += 16
            tok_v = qg.n
            for wc in range(NWC):
                ps = PS4 if wc % 2 == 0 else PS5
                @TE
                def _(t, wc=wc, ps=ps, tok_v=tok_v, sg=sdv.n):
                    t.wait_ge(qg.h, tok_v)
                    t.wait_ge(sdv.h, t_vones)
                    if wc >= 2:
                        t.wait_ge(sdv.h, sdv.n - 1)
                    else:
                        t.wait_ge(sdv.h, sg)
                    for ck in range(NDC):
                        inst = t.matmul(
                            ps[:], XT1[:, ck * WIN + wc * 128: ck * WIN + (wc + 1) * 128],
                            WPT[0][:, ck * 512:(ck + 1) * 512],
                            start=(ck == 0), stop=(ck == NDC - 1))
                    spe.inc(inst, 1)
                t_mm = spe.n
                @VE
                def _(v, wc=wc, ps=ps, t_mm=t_mm):
                    v.wait_ge(spe.h, t_mm)
                    inst = v.tensor_copy(
                        Vg.rearrange("p (c f) -> p c f", c=NWC)[:, wc].rearrange(
                            "p (h d) -> p h d", h=8)[:, :, 0:64],
                        ps.ap().rearrange("p (h d) -> p h d", h=8))
                    sdv.inc(inst, 1)
            wpt_free[0] = sdv.n
            t_kv = sdv.n

            for lh in range(8):
                gh = gi * 8 + lh
                pr, side = lh // 2, lh % 2
                r0 = side * 64
                for ti in range(2):
                    jcs = list(range(4 * ti, 4 * ti + 7))
                    psAb = [PS2, PS3, PS6][(lh * 2 + ti) % 3]
                    psAi = (lh * 2 + ti) % 3
                    psA = psAb.ap()[0:65, :]
                    exp_tok = [0] * len(jcs)
                    for idx, jc in enumerate(jcs):
                        sb = PS0 if idx % 2 == 0 else PS1
                        @TE
                        def _(t, gh=gh, pr=pr, r0=r0, ti=ti, jc=jc, sb=sb, idx=idx,
                              psAi=psAi,
                              pf0=ps01_free[0], pf1=ps01_free[1]):
                            t.wait_ge(sdv.h, t_kv)
                            t.wait_ge(sdv.h, t_crow)
                            if idx == 0:
                                t.wait_ge(sdv.h, psA_free[psAi])
                            t.wait_ge(sac.h, (pf0 if idx % 2 == 0 else pf1))
                            t.matmul(
                                sb[:],
                                Kg[r0:r0 + 64, pr * WIN + jc * 128: pr * WIN + (jc + 1) * 128],
                                Qg[r0:r0 + 64, pr * QN + ti * 512: pr * QN + (ti + 1) * 512],
                                start=True, stop=False)
                            inst = t.matmul(
                                sb[:], SELS[:, gh * 128:(gh + 1) * 128],
                                CROWB[0:32, ti * 512:(ti + 1) * 512],
                                start=False, stop=True)
                            spe.inc(inst, 1)
                        t_s = spe.n
                        pt = PT[idx % 2]
                        @SC
                        def _(s, sb=sb, pt=pt, gh=gh, ti=ti, jc=jc, t_s=t_s):
                            s.wait_ge(spe.h, t_s)
                            inst = s.activation(
                                out=pt[:], in_=sb[:], func=AF.Exp,
                                bias=NEGC[:, jc * H + gh: jc * H + gh + 1],
                                scale=f32(1.0))
                            sac.inc(inst, 1)
                        exp_tok[idx] = sac.n
                        ps01_free[idx % 2] = sac.n
                        mbase = NPAD + 512 * ti - 128 * jc
                        if mbase < 127:
                            @GP
                            def _(g, pt=pt, mbase=mbase, tok=sac.n):
                                g.wait_ge(sac.h, tok)
                                inst = g.affine_select(
                                    out=pt[:], in_=pt[:], pattern=[[1, 512]],
                                    compare_op=ALU.is_ge, fill=0.0, base=mbase,
                                    channel_multiplier=-1)
                                spo.inc(inst, 1)
                            av_sem, av_tok = spo, spo.n
                        else:
                            av_sem, av_tok = sac, sac.n
                        @TE
                        def _(t, jc=jc, pt=pt, psA=psA, lh=lh, idx=idx,
                              sh=av_sem.h, tok=av_tok, n=len(jcs)):
                            t.wait_ge(sh, tok)
                            inst = t.matmul(psA,
                                            Vg[:, jc * 520 + lh * 65: jc * 520 + lh * 65 + 65],
                                            pt[:], start=(idx == 0), stop=(idx == n - 1))
                            spe.inc(inst, 1)
                    t_av = spe.n
                    @VE
                    def _(v, psAb=psAb, t_av=t_av):
                        v.wait_ge(spe.h, t_av)
                        inst = v.reciprocal(RRB[64:65, :], psAb.ap()[64:65, :])
                        sdv.inc(inst, 1)
                    t_rr = sdv.n
                    @TE
                    def _(t, t_rr=t_rr):
                        t.wait_ge(sdv.h, t_rr)
                        inst = t.matmul(PS4.ap()[0:64, :], ONESR64[64:65, 0:64], RRB[64:65, :],
                                        start=True, stop=True)
                        spe.inc(inst, 1)
                    t_rb = spe.n
                    @VE
                    def _(v, t_rb=t_rb):
                        v.wait_ge(spe.h, t_rb)
                        inst = v.tensor_copy(RBC[:], PS4.ap()[0:64, :])
                        sdv.inc(inst, 1)
                    if side == 0:
                        @VE
                        def _(v, psAb=psAb, gh=gh, ti=ti):
                            inst = v.tensor_tensor(
                                out=ATT[0:64, (gh // 2) * QN + ti * 512: (gh // 2) * QN + (ti + 1) * 512],
                                in0=psAb.ap()[0:64, :], in1=RBC[:], op=ALU.mult)
                            sdv.inc(inst, 1)
                        psA_free[psAi] = sdv.n
                    else:
                        oi = ti
                        @VE
                        def _(v, psAb=psAb, oi=oi, t_dma=oscr_qg[oi]):
                            v.wait_ge(qg.h, t_dma)
                            inst = v.tensor_tensor(out=OSC[oi][:], in0=psAb.ap()[0:64, :],
                                                   in1=RBC[:], op=ALU.mult)
                            sdv.inc(inst, 1)
                        psA_free[psAi] = sdv.n
                        t_os = sdv.n
                        @GP
                        def _(g, oi=oi, gh=gh, ti=ti, t_os=t_os):
                            g.wait_ge(sdv.h, t_os)
                            g.dma_start(
                                out=ATT[64:128, (gh // 2) * QN + ti * 512: (gh // 2) * QN + (ti + 1) * 512],
                                in_=OSC[oi][:]).then_inc(qg.h, 16)
                        qg.n += 16
                        oscr_qg[oi] = qg.n
        t_att_v = sdv.n
        t_att_g = qg.n

        if debug:
            @SC
            def _(s):
                s.wait_ge(sdv.h, t_att_v)
                s.wait_ge(qg.h, t_att_g)
                s.dma_start(out=dbg["logf"], in_=LOGF[:]).then_inc(sst.h, 16)
                s.dma_start(out=dbg["ccol"], in_=CCOL[:]).then_inc(sst.h, 16)
                s.dma_start(out=dbg["att"], in_=ATT[:]).then_inc(sst.h, 16)
            sst.n += 48

        emit_on[0] = phases >= 4
        wot_free = [0, 0]
        for dc in range(NDC if p3o else 0):
            slot = dc % 2
            @GP
            def _(g, dc=dc, slot=slot):
                g.wait_ge(sdv.h, wot_free[slot])
                inst = g.dma_start(
                    out=WOTB[slot].rearrange("p (c n) -> p c n", c=NDC),
                    in_=woT.rearrange("(c p) n -> p c n", p=128)[:, :, dc * 128:(dc + 1) * 128])
                inst.then_inc(qg.h, 16)
            qg.n += 16
            tok_w = qg.n
            for ti in range(2):
                ps = PS0 if ti == 0 else PS1
                @TE
                def _(t, dc=dc, ti=ti, ps=ps, slot=slot, tok_w=tok_w, ag=sac.n):
                    t.wait_ge(qg.h, tok_w)
                    t.wait_ge(sdv.h, t_att_v)
                    t.wait_ge(qg.h, t_att_g)
                    t.wait_ge(sac.h, ag)
                    if dc + ti > 0:
                        t.wait_ge(sdv.h, sdv.n - 1)
                    for ck in range(NDC):
                        inst = t.matmul(ps[:],
                                        WOTB[slot][:, ck * 128:(ck + 1) * 128],
                                        ATT[:, ck * QN + ti * 512: ck * QN + ti * 512 + 512],
                                        start=(ck == 0), stop=(ck == NDC - 1))
                    spe.inc(inst, 1)
                t_o = spe.n
                @VE
                def _(v, dc=dc, ti=ti, ps=ps, t_o=t_o):
                    v.wait_ge(spe.h, t_o)
                    inst = v.tensor_copy(OTB[:, dc * QN + ti * 512: dc * QN + (ti + 1) * 512], ps[:])
                    sdv.inc(inst, 1)
            wot_free[slot] = sdv.n
        t_otb = sdv.n

        xt2_tok = [0] * NQC
        t_prev_x2 = 0
        out2i_qg = 0
        for qc in range(min(NQC, p3n)):
            wc = qc + 3
            buf = qc % 2
            @SY
            def _(q, qc=qc, wc=wc, buf=buf):
                q.wait_ge(sdv.h, xw_free[buf])
                if qc < 2:
                    q.wait_ge(sdv.h, t_att_v)
                q.dma_start(out=XW[buf][:], in_=xw_t[wc]).then_inc(qs.h, 16)
            qs.n += 16
            t_x = qs.n

            for ck in range(NDC):
                @TE
                def _(t, qc=qc, ck=ck, t_prev_x2=t_prev_x2):
                    t.wait_ge(sdv.h, t_otb)
                    t.wait_ge(sdv.h, t_prev_x2)
                    if ck >= 2:
                        t.wait_ge(sdv.h, sdv.n - 1)
                    inst = t.transpose(PBT[ck % 2],
                                       OTB[:, ck * QN + qc * 128: ck * QN + qc * 128 + 128],
                                       IDENTB[:])
                    spe.inc(inst, 1)
                t_tr = spe.n
                @VE
                def _(v, qc=qc, ck=ck, buf=buf, t_tr=t_tr, t_x=t_x):
                    v.wait_ge(spe.h, t_tr)
                    v.wait_ge(qs.h, t_x)
                    inst = v.tensor_tensor(
                        out=X2S[:, ck * 128:(ck + 1) * 128],
                        in0=XW[buf][:, ck * 128:(ck + 1) * 128],
                        in1=PBT[ck % 2],
                        op=ALU.add)
                    sdv.inc(inst, 1)
            xw_free[buf] = sdv.n
            t_x2 = sdv.n

            if debug:
                @SC
                def _(s, qc=qc, t_x2=t_x2):
                    s.wait_ge(sdv.h, t_x2)
                    s.dma_start(out=dbg["x2"].rearrange("(c p) d -> c p d", p=128)[qc],
                                in_=X2S[:]).then_inc(sst.h, 16)
                sst.n += 16

            @SC
            def _(s, t_x2=t_x2):
                s.wait_ge(sdv.h, t_x2)
                inst = s.activation(out=SQS[:], in_=X2S[:], func=AF.Square, bias=f32(0.0), scale=f32(1.0))
                sac.inc(inst, 1)
            t_sq = sac.n
            t_xln2 = emit_ln_chain(
                stats_waits=[(sac.h, t_sq)],
                apply_waits=([(spe.h, spe.n)] if qc >= 1 else []),
                src=X2S)

            for gch in range(NDC):
                ck, nib = gch // 2, gch % 2
                @TE
                def _(t, qc=qc, gch=gch, ck=ck, nib=nib, t_xln2=t_xln2):
                    t.wait_ge(sdv.h, t_xln2)
                    if gch >= 2:
                        t.wait_ge(sdv.h, sdv.n - 1)
                    inst = t.transpose(PBT[gch % 2],
                                       XLN[:, 256 * ck + nib: 256 * ck + nib + 255: 2],
                                       IDENTB[:])
                    spe.inc(inst, 1)
                t_tr = spe.n
                @VE
                def _(v, qc=qc, gch=gch, t_tr=t_tr):
                    v.wait_ge(spe.h, t_tr)
                    inst = v.tensor_copy(XT2[:, gch * QN + qc * 128: gch * QN + qc * 128 + 128],
                                         PBT[gch % 2])
                    sdv.inc(inst, 1)
            xt2_tok[qc] = sdv.n

            for ck in range(NDC):
                @TE
                def _(t, qc=qc, ck=ck):
                    t.wait_ge(sdv.h, xt2_tok[qc])
                    if ck >= 2:
                        t.wait_ge(sdv.h, sdv.n - 1)
                    inst = t.transpose(PFT[ck % 2], X2S[:, ck * 128:(ck + 1) * 128], IDENTF[:])
                    spe.inc(inst, 1)
                t_tr = spe.n
                @VE
                def _(v, ck=ck, t_tr=t_tr, t_o2p=out2i_qg):
                    v.wait_ge(spe.h, t_tr)
                    if ck == 0:
                        v.wait_ge(qg.h, t_o2p)
                    inst = v.tensor_copy(OUTS[:, ck * 128:(ck + 1) * 128], PFT[ck % 2])
                    sdv.inc(inst, 1)
            t_prev_x2 = sdv.n
            t_o2i = sdv.n
            @GP
            def _(g, qc=qc, t_o2i=t_o2i):
                g.wait_ge(sdv.h, t_o2i)
                inst = g.dma_start(
                    out=out2d.rearrange("(c p) w -> p c w", p=128)[:, :, qc * 128:(qc + 1) * 128],
                    in_=OUTS.rearrange("p (c w) -> p c w", c=NDC))
                inst.then_inc(qg.h, 16)
            qg.n += 16
            out2i_qg = qg.n
        t_out2init = qg.n
        t_xt2_all = xt2_tok[NQC - 1]

        if debug:
            @SC
            def _(s):
                s.wait_ge(sdv.h, t_xt2_all)
                s.dma_start(out=dbg["xt2"], in_=XT2[:]).then_inc(sst.h, 16)
            sst.n += 16

        emit_on[0] = phases >= 5
        pk_free = [0, 0, 0, 0]
        slotc = [0]
        dec_par = [0]
        dec_gp = [0, 0]

        def decode_chunk(pk_ap, sc_ap, dst_both, wd):
            A, B, AB = slice(0, wd), slice(wd, 2 * wd), slice(0, 2 * wd)
            par = dec_par[0] % 2
            dec_par[0] += 1
            DF = DF0 if par == 0 else DF0B
            @VE
            def _(v, DF=DF, war=dec_gp[par]):
                v.wait_ge(spo.h, war)
                v.tensor_scalar(out=DF[:, A], in0=pk_ap, scalar1=7, scalar2=6,
                                op0=ALU.bitwise_and, op1=ALU.logical_shift_left)
                v.tensor_scalar(out=DF[:, B], in0=pk_ap, scalar1=0x70, scalar2=2,
                                op0=ALU.bitwise_and, op1=ALU.logical_shift_left)
                v.tensor_scalar(out=DSGN[:, A], in0=pk_ap, scalar1=8, scalar2=12,
                                op0=ALU.bitwise_and, op1=ALU.logical_shift_left)
                v.tensor_scalar(out=DSGN[:, B], in0=pk_ap, scalar1=128, scalar2=8,
                                op0=ALU.bitwise_and, op1=ALU.logical_shift_left)
                v.tensor_single_scalar(out=DF[:, AB], in_=DF[:, AB], scalar=0x3F00, op=ALU.add)
                v.tensor_scalar(out=DF1.bitcast(BF16)[:, AB], in0=DF.bitcast(BF16)[:, AB],
                                scalar1=2.0, scalar2=-1.0, op0=ALU.mult, op1=ALU.add)
                v.tensor_tensor(out=DF.bitcast(BF16)[:, AB], in0=DF.bitcast(BF16)[:, AB],
                                in1=DF1.bitcast(BF16)[:, AB], op=ALU.min)
                inst = v.tensor_tensor(out=DF[:, AB], in0=DF[:, AB], in1=DSGN[:, AB],
                                       op=ALU.bitwise_or)
                sdv.inc(inst, 1)
            t_dec = sdv.n
            @GP
            def _(g, DF=DF, t_dec=t_dec):
                g.wait_ge(sdv.h, t_dec)
                g.tensor_tensor(out=dst_both[:, 0:wd], in0=DF.bitcast(BF16)[:, A],
                                in1=sc_ap, op=ALU.mult)
                inst = g.tensor_tensor(out=dst_both[:, wd:2 * wd], in0=DF.bitcast(BF16)[:, B],
                                       in1=sc_ap, op=ALU.mult)
                spo.inc(inst, 1)
            dec_gp[par] = spo.n
            return spo.n

        gw_free = [0, 0]
        uw_free = [0, 0]
        gw_tok = [0, 0]
        uw_tok = [0, 0]
        inter_last = 0
        dwn_free = [0, 0]

        def load_decode_window(which, w, buf):
            src_pk = gpk if which == "g" else upk
            src_sc = gsc if which == "g" else usc
            dstT = (GW if which == "g" else UW)[buf]
            freetok = (gw_free if which == "g" else uw_free)[buf]
            last = 0
            for ck in range(8):
                slot = slotc[0] % 4
                slotc[0] += 1
                @SY
                def _(q, w=w, ck=ck, slot=slot, freetok=freetok, src_pk=src_pk, src_sc=src_sc):
                    q.wait_ge(spo.h, pk_free[slot])
                    q.wait_ge(spe.h, freetok)
                    q.dma_start(out=PKB[slot][:],
                                in_=src_pk[w * 512:(w + 1) * 512, ck * 128:(ck + 1) * 128],
                                transpose=True).then_inc(qs.h, 16)
                    q.dma_start(out=SCB[slot][:],
                                in_=src_sc[ck * 128:(ck + 1) * 128, w * 512:(w + 1) * 512]
                                ).then_inc(qs.h, 16)
                qs.n += 32
                t_pk = qs.n
                @VE
                def _(v, t_pk=t_pk):
                    v.wait_ge(qs.h, t_pk)
                last = decode_chunk(PKB[slot][:], SCB[slot][:],
                                    dstT[:, (2 * ck) * 512:(2 * ck + 2) * 512], 512)
                pk_free[slot] = last
            if which == "g":
                gw_tok[buf] = last
            else:
                uw_tok[buf] = last

        psGU_free = [0, 0, 0, 0]
        for s_ in range(NSUP):
            for wv in range(4):
                w = s_ * 4 + wv
                buf = w % 2
                load_decode_window("g", w, buf)
                load_decode_window("u", w, buf)
                for nch in range(4):
                    n_in_sup = wv * 4 + nch
                    for ti in range(2):
                        par = (nch * 2 + ti) % 2
                        psG = PS0 if par == 0 else PS1
                        psU = PS2 if par == 0 else PS3
                        @TE
                        def _(t, w=w, buf=buf, nch=nch, ti=ti, psG=psG, psU=psU, par=par, sg=sdv.n):
                            t.wait_ge(sdv.h, t_xt2_all)
                            if w == 0 and nch == 0 and ti == 0:
                                t.wait_ge(sdv.h, sg)
                            t.wait_ge(spo.h, gw_tok[buf])
                            t.wait_ge(spo.h, uw_tok[buf])
                            t.wait_ge(sdv.h, psGU_free[par])
                            t.wait_ge(sdv.h, psGU_free[2 + par])
                            for g16 in range(16):
                                inst = t.matmul(
                                    psG[:], GW[buf][:, g16 * 512 + nch * 128: g16 * 512 + (nch + 1) * 128],
                                    XT2[:, g16 * QN + ti * 512: g16 * QN + (ti + 1) * 512],
                                    start=(g16 == 0), stop=(g16 == 15))
                            spe.inc(inst, 1)
                            for g16 in range(16):
                                inst = t.matmul(
                                    psU[:], UW[buf][:, g16 * 512 + nch * 128: g16 * 512 + (nch + 1) * 128],
                                    XT2[:, g16 * QN + ti * 512: g16 * QN + (ti + 1) * 512],
                                    start=(g16 == 0), stop=(g16 == 15))
                            spe.inc(inst, 1)
                        t_gu = spe.n
                        @VE
                        def _(v, psG=psG, t_gu=t_gu):
                            v.wait_ge(spe.h, t_gu)
                            v.tensor_copy(GF[:], psG[:])
                            v.tensor_tensor(out=SQT[:], in0=GF[:], in1=GF[:], op=ALU.mult)
                            v.tensor_scalar(out=YT[:], in0=SQT[:], scalar1=float(GELU_A), scalar2=float(GELU_B),
                                            op0=ALU.mult, op1=ALU.add)
                            inst = v.tensor_tensor(out=U2T[:], in0=YT[:], in1=GF[:], op=ALU.mult)
                            sdv.inc(inst, 1)
                        t_u2 = sdv.n
                        psGU_free[par] = t_u2
                        @SC
                        def _(s, t_u2=t_u2):
                            s.wait_ge(sdv.h, t_u2)
                            inst = s.activation(out=THT[:], in_=U2T[:], func=AF.Tanh,
                                                bias=f32(0.0), scale=f32(1.0))
                            sac.inc(inst, 1)
                        t_th = sac.n
                        @VE
                        def _(v, t_th=t_th):
                            v.wait_ge(sac.h, t_th)
                            inst = v.tensor_scalar(out=QT[:], in0=THT[:], scalar1=0.5,
                                                   scalar2=0.5, op0=ALU.mult, op1=ALU.add)
                            sdv.inc(inst, 1)
                        t_qt = sdv.n
                        @GP
                        def _(g, t_qt=t_qt):
                            g.wait_ge(sdv.h, t_qt)
                            inst = g.tensor_tensor(out=T1T[:], in0=QT[:], in1=GF[:], op=ALU.mult)
                            spo.inc(inst, 1)
                        t_t1 = spo.n
                        @VE
                        def _(v, psU=psU, n_in_sup=n_in_sup, ti=ti, t_t1=t_t1):
                            v.wait_ge(spo.h, t_t1)
                            inst = v.tensor_tensor(
                                out=INTER[:, n_in_sup * QN + ti * 512: n_in_sup * QN + (ti + 1) * 512],
                                in0=T1T[:], in1=psU[:], op=ALU.mult)
                            sdv.inc(inst, 1)
                        psGU_free[2 + par] = sdv.n
                        inter_last = sdv.n
                gw_free[buf] = spe.n
                uw_free[buf] = spe.n

            psD_free = [0, 0]
            if s_ == 0:
                dstg_qg = [0, 0]
            for hw in range(8):
                dbuf = hw % 2
                dw_last = 0
                for ck in range(8):
                    slot = slotc[0] % 4
                    slotc[0] += 1
                    @SY
                    def _(q, s_=s_, hw=hw, ck=ck, slot=slot, dbuf=dbuf):
                        q.wait_ge(spo.h, pk_free[slot])
                        q.wait_ge(spe.h, dwn_free[dbuf])
                        q.dma_start(out=PKB[slot][0:128, 0:256],
                                    in_=dpk[hw * 256:(hw + 1) * 256,
                                            s_ * 1024 + ck * 128: s_ * 1024 + (ck + 1) * 128],
                                    transpose=True).then_inc(qs.h, 16)
                        q.dma_start(out=SCB[slot][0:128, 0:256],
                                    in_=dsc[s_ * 1024 + ck * 128: s_ * 1024 + (ck + 1) * 128,
                                            hw * 256:(hw + 1) * 256]).then_inc(qs.h, 16)
                    qs.n += 32
                    t_pk = qs.n
                    @VE
                    def _(v, t_pk=t_pk):
                        v.wait_ge(qs.h, t_pk)
                    dw_last = decode_chunk(PKB[slot][0:128, 0:256], SCB[slot][0:128, 0:256],
                                           DWNW[dbuf][:, (2 * ck) * 256:(2 * ck + 2) * 256], 256)
                    pk_free[slot] = dw_last
                    if debug and s_ == NSUP - 1 and hw == 7 and ck == 7:
                        @SC
                        def _(s, slot=slot, dw_last=dw_last):
                            s.wait_ge(sdv.h, dw_last)
                            s.dma_start(out=dbg["dpkb"], in_=PKB[slot][0:128, 0:256]).then_inc(sst.h, 16)
                            s.dma_start(out=dbg["dscb"], in_=SCB[slot][0:128, 0:256]).then_inc(sst.h, 16)
                            s.dma_start(out=dbg["dwnw"], in_=DWNW[1][:]).then_inc(sst.h, 16)
                        sst.n += 48
                for hc2 in range(2):
                    hcg = hw * 2 + hc2
                    for ti in range(2):
                        par = (hc2 * 2 + ti) % 2
                        ps = PS4 if par == 0 else PS5
                        @TE
                        def _(t, hc2=hc2, ti=ti, ps=ps, dbuf=dbuf, dw_last=dw_last, par=par):
                            t.wait_ge(spo.h, dw_last)
                            t.wait_ge(sdv.h, inter_last)
                            t.wait_ge(sdv.h, psD_free[par])
                            for kc in range(16):
                                inst = t.matmul(
                                    ps[:],
                                    DWNW[dbuf][:, kc * 256 + hc2 * 128: kc * 256 + (hc2 + 1) * 128],
                                    INTER[:, kc * QN + ti * 512: kc * QN + (ti + 1) * 512],
                                    start=(kc == 0), stop=(kc == 15))
                            spe.inc(inst, 1)
                        t_dn = spe.n
                        dstg = DSTG[par]
                        @VE
                        def _(v, ps=ps, dstg=dstg, t_dn=t_dn, t_dma=dstg_qg[par]):
                            v.wait_ge(spe.h, t_dn)
                            v.wait_ge(qg.h, t_dma)
                            inst = v.tensor_copy(dstg[:], ps[:])
                            sdv.inc(inst, 1)
                        psD_free[par] = sdv.n
                        t_ds = sdv.n
                        @GP
                        def _(g, s_=s_, hcg=hcg, ti=ti, dstg=dstg, t_ds=t_ds):
                            g.wait_ge(sdv.h, t_ds)
                            if s_ == 0:
                                g.wait_ge(qg.h, t_out2init)
                            else:
                                g.wait_ge(qg.h, qg.n)
                            inst = g.dma_start(
                                out=out2d[hcg * 128:(hcg + 1) * 128, ti * 512:(ti + 1) * 512],
                                in_=dstg[:], accum_op=ALU.add)
                            inst.then_inc(qg.h, 16)
                        qg.n += 16
                        dstg_qg[par] = qg.n
                dwn_free[dbuf] = spe.n

        t_ffn_done = qg.n

        if debug:
            @SC
            def _(s):
                s.wait_ge(sdv.h, inter_last)
                s.dma_start(out=dbg["inter"], in_=INTER[:, 0:4 * QN]).then_inc(sst.h, 16)
                s.dma_start(out=dbg["gw"], in_=GW[0][:]).then_inc(sst.h, 16)
                s.wait_ge(qg.h, t_ffn_done)
                s.dma_start(out=dbg["out2d"], in_=out2d).then_inc(sst.h, 16)
                s.dma_start(out=dbg["dstg0"], in_=DSTG[0][:]).then_inc(sst.h, 16)
                s.dma_start(out=dbg["dstg1"], in_=DSTG[1][:]).then_inc(sst.h, 16)
            sst.n += 80

        emit_on[0] = phases >= 6
        t_prev_f = 0
        for qc in range(NQC):
            @SY
            def _(q, qc=qc, t_prev_f=t_prev_f):
                q.wait_ge(qg.h, t_ffn_done)
                q.wait_ge(sdv.h, t_prev_f)
                q.dma_start(out=X2S[:],
                            in_=out2d.rearrange("(c p) w -> p c w", p=128)[:, :, qc * 128:(qc + 1) * 128]
                            ).then_inc(qs.h, 16)
            qs.n += 16
            t_l = qs.n
            for ck in range(NDC):
                @TE
                def _(t, ck=ck, t_l=t_l, sg=sdv.n):
                    t.wait_ge(qs.h, t_l)
                    if ck >= 2:
                        t.wait_ge(sdv.h, sdv.n - 1)
                    else:
                        t.wait_ge(sdv.h, sg)
                    inst = t.transpose(PFT[ck % 2], X2S[:, ck * 128:(ck + 1) * 128], IDENTF[:])
                    spe.inc(inst, 1)
                t_tr = spe.n
                @VE
                def _(v, ck=ck, t_tr=t_tr, t_stp=sst.n):
                    v.wait_ge(spe.h, t_tr)
                    if ck == 0:
                        v.wait_ge(sst.h, t_stp)
                    inst = v.tensor_copy(OUTS[:, ck * 128:(ck + 1) * 128], PFT[ck % 2])
                    sdv.inc(inst, 1)
            t_of = sdv.n
            t_prev_f = t_of
            @SC
            def _(s, qc=qc, t_of=t_of):
                s.wait_ge(sdv.h, t_of)
                s.dma_start(out=out.rearrange("(c p) d -> c p d", p=128)[qc],
                            in_=OUTS[:]).then_inc(sst.h, 16)
            sst.n += 16

        @SC
        def _(s):
            s.wait_ge(sst.h, sst.n)

    return nc


def _perm_mlp():
    idx = np.arange(MLP).reshape(64, 256)
    return np.concatenate([idx[:, 0::2], idx[:, 1::2]], axis=1).reshape(-1)


def _prep_inputs(inp):
    x = np.ascontiguousarray(np.asarray(inp["L0_attn_x"], dtype=np.float32))
    Wq = np.asarray(inp["L0_attn_Wq"], dtype=np.float32)
    Wk = np.asarray(inp["L0_attn_Wk"], dtype=np.float32)
    Wv = np.asarray(inp["L0_attn_Wv"], dtype=np.float32)
    Wf = np.asarray(inp["L0_attn_Wf"], dtype=np.float32)
    bf_ = np.asarray(inp["L0_attn_bf"], dtype=np.float32)
    Wo = np.asarray(inp["L0_attn_Wo"], dtype=np.float32)
    gW = np.asarray(inp["L0_ffn_gate_W_packed"])
    gS = np.asarray(inp["L0_ffn_gate_scales"], dtype=np.float32)
    uW = np.asarray(inp["L0_ffn_up_W_packed"])
    uS = np.asarray(inp["L0_ffn_up_scales"], dtype=np.float32)
    dW = np.asarray(inp["L0_ffn_down_W_packed"])
    dS = np.asarray(inp["L0_ffn_down_scales"], dtype=np.float32)

    perm = _perm_mlp()
    g_u16 = np.ascontiguousarray(
        np.ascontiguousarray(gW.astype('<i4')).view(np.uint16).reshape(MLP, D // 2, 2)[:, :, 0][perm])
    u_u16 = np.ascontiguousarray(
        np.ascontiguousarray(uW.astype('<i4')).view(np.uint16).reshape(MLP, D // 2, 2)[:, :, 0][perm])
    d_u16 = np.ascontiguousarray(
        np.ascontiguousarray(dW.astype('<i4')).view(np.uint16).reshape(D, MLP // 2, 2)[:, :, 0])
    gscT = np.ascontiguousarray(np.repeat(gS[perm].T, 16, axis=0))
    uscT = np.ascontiguousarray(np.repeat(uS[perm].T, 16, axis=0))
    dscT = np.ascontiguousarray(np.repeat(dS.T, 16, axis=0))

    shared = dict(
        wqT=np.ascontiguousarray(Wq.T), wkT=np.ascontiguousarray(Wk.T),
        wvT=np.ascontiguousarray(Wv.T), woT=np.ascontiguousarray(Wo.T),
        wfT=np.ascontiguousarray(Wf.T), bfr=np.ascontiguousarray(bf_.reshape(1, H)),
        gpk=g_u16, upk=u_u16, dpk=d_u16, gsc=gscT, usc=uscT, dsc=dscT)

    in_maps = []
    for c in range(8):
        b, half = c // 2, c % 2
        if half == 0:
            xwc = np.concatenate([np.zeros((NPAD, D), np.float32), x[b, 0:QN]], axis=0)
            km = np.zeros((NWC, 128), np.float32)
            km.reshape(-1)[:NPAD] = NEG
        else:
            xwc = np.ascontiguousarray(x[b, T - WIN:T])
            km = np.zeros((NWC, 128), np.float32)
        m = dict(shared)
        m["xw"] = xwc
        m["km"] = np.ascontiguousarray(km.T)
        in_maps.append(m)
    return in_maps


def kernel(**inp):
    in_maps = _prep_inputs(inp)
    nc = _build(debug=False)
    res = run_bass_kernel_spmd(nc, in_maps, core_ids=list(range(8)))
    outp = np.empty((B, T, D), np.float32)
    for c in range(8):
        b, half = c // 2, c % 2
        outp[b, half * QN:(half + 1) * QN] = res.results[c]["out"]
    return outp


# version 15
# speedup vs baseline: 1.0923x; 1.0923x over previous
import numpy as np
import ml_dtypes
import concourse.bass as bass
import concourse.mybir as mybir
from concourse.ap import AP
from concourse.bass_utils import run_bass_kernel_spmd

F32 = mybir.dt.float32
BF16 = mybir.dt.bfloat16
U16 = mybir.dt.uint16
I32 = mybir.dt.int32
ALU = mybir.AluOpType
AF = mybir.ActivationFunctionType
AXX = mybir.AxisListType.X
f32 = np.float32

B, T, D, H, DH = 4, 2048, 2048, 32, 64
WIN = 1408
QN = 1024
NPAD = 384
NWC = WIN // 128
NQC = QN // 128
NDC = D // 128
MLP = 16384
NSUP = 8
NEG = -1.0e30
GELU_A = f32(0.7978845608028654 * 0.044715)
GELU_B = f32(0.7978845608028654)
RSQRT_MAGIC = 0x5F3759DF
LN1P_C = [1.0, -0.49999502, 0.3331927, -0.24844374, 0.19111443,
          -0.13674787, 0.07836182, -0.02958858, 0.00525347]


def _build(debug=False, detect_races=True, phases=9, p1n=99, p1tr=True, p1ln=True, p3o=True, p3n=99):
    nc = bass.Bass("TRN2", target_bir_lowering=False, debug=False,
                   detect_race_conditions=detect_races)

    xw = nc.dram_tensor("xw", [WIN, D], F32, kind="ExternalInput").ap()
    km_in = nc.dram_tensor("km", [128, NWC], F32, kind="ExternalInput").ap()
    wqT = nc.dram_tensor("wqT", [D, D], F32, kind="ExternalInput").ap()
    wkT = nc.dram_tensor("wkT", [D, D], F32, kind="ExternalInput").ap()
    wvT = nc.dram_tensor("wvT", [D, D], F32, kind="ExternalInput").ap()
    woT = nc.dram_tensor("woT", [D, D], F32, kind="ExternalInput").ap()
    wfT_in = nc.dram_tensor("wfT", [D, H], F32, kind="ExternalInput").ap()
    bfr_in = nc.dram_tensor("bfr", [1, H], F32, kind="ExternalInput").ap()
    gpk = nc.dram_tensor("gpk", [MLP, D // 2], U16, kind="ExternalInput").ap()
    upk = nc.dram_tensor("upk", [MLP, D // 2], U16, kind="ExternalInput").ap()
    dpk = nc.dram_tensor("dpk", [D, MLP // 2], U16, kind="ExternalInput").ap()
    gsc = nc.dram_tensor("gsc", [D // 2, MLP], F32, kind="ExternalInput").ap()
    usc = nc.dram_tensor("usc", [D // 2, MLP], F32, kind="ExternalInput").ap()
    dsc = nc.dram_tensor("dsc", [MLP // 2, D], F32, kind="ExternalInput").ap()
    out = nc.dram_tensor("out", [QN, D], F32, kind="ExternalOutput").ap()

    out2d = nc.dram_tensor("out2d", [D, QN], F32, kind="Internal").ap()

    dbg = {}
    if debug:
        dbg["xt1"] = nc.dram_tensor("dbg_xt1", [128, NDC * WIN], BF16, kind="ExternalOutput").ap()
        dbg["logf"] = nc.dram_tensor("dbg_logf", [128, NWC * H], F32, kind="ExternalOutput").ap()
        dbg["ccol"] = nc.dram_tensor("dbg_ccol", [128, NWC * H], F32, kind="ExternalOutput").ap()
        dbg["att"] = nc.dram_tensor("dbg_att", [128, NDC * QN], BF16, kind="ExternalOutput").ap()
        dbg["x2"] = nc.dram_tensor("dbg_x2", [QN, D], F32, kind="ExternalOutput").ap()
        dbg["xt2"] = nc.dram_tensor("dbg_xt2", [128, NDC * QN], BF16, kind="ExternalOutput").ap()
        dbg["inter"] = nc.dram_tensor("dbg_inter", [128, 4 * QN], BF16, kind="ExternalOutput").ap()
        dbg["gw"] = nc.dram_tensor("dbg_gw", [128, 16 * 512], BF16, kind="ExternalOutput").ap()
        dbg["out2d"] = nc.dram_tensor("dbg_out2d", [2048, 1024], F32, kind="ExternalOutput").ap()
        dbg["dwnw"] = nc.dram_tensor("dbg_dwnw", [128, 4096], BF16, kind="ExternalOutput").ap()
        dbg["dstg0"] = nc.dram_tensor("dbg_dstg0", [128, 512], F32, kind="ExternalOutput").ap()
        dbg["dstg1"] = nc.dram_tensor("dbg_dstg1", [128, 512], F32, kind="ExternalOutput").ap()
        dbg["dpkb"] = nc.dram_tensor("dbg_dpkb", [128, 256], U16, kind="ExternalOutput").ap()
        dbg["dscb"] = nc.dram_tensor("dbg_dscb", [128, 256], F32, kind="ExternalOutput").ap()
        dbg["xln2c"] = nc.dram_tensor("dbg_xln2c", [128, D], BF16, kind="ExternalOutput").ap()
        dbg["xw2c"] = nc.dram_tensor("dbg_xw2c", [128, D], F32, kind="ExternalOutput").ap()
        dbg["stat2c"] = nc.dram_tensor("dbg_stat2c", [128, 16], F32, kind="ExternalOutput").ap()

    identf = nc.inline_tensor(np.eye(128, dtype=np.float32), "identf")
    identb = nc.inline_tensor(np.eye(128, dtype=ml_dtypes.bfloat16), "identb")

    from contextlib import ExitStack
    with ExitStack() as st_:
        e = st_.enter_context

        STAT = e(nc.sbuf_tensor("STAT", [128, 16], F32))
        KM = e(nc.sbuf_tensor("KM", [128, 16], F32))
        LOGF = e(nc.sbuf_tensor("LOGF", [128, NWC * H], F32))
        LTMP = e(nc.sbuf_tensor("LTMP", [128, H], F32))
        LTM2 = e(nc.sbuf_tensor("LTM2", [128, H], F32))
        UALL = e(nc.sbuf_tensor("UALL", [128, NWC * H], F32))
        HORN = e(nc.sbuf_tensor("HORN", [128, NWC * H], F32))
        CCOL = e(nc.sbuf_tensor("CCOL", [128, NWC * H], F32))
        NEGC = e(nc.sbuf_tensor("NEGC", [128, NWC * H], F32))
        TRIF = e(nc.sbuf_tensor("TRIF", [128, 128], F32))
        ONESF = e(nc.sbuf_tensor("ONESF", [128, 128], F32))
        ONESRF = e(nc.sbuf_tensor("ONESRF", [1, 128], F32))
        ONESRB = e(nc.sbuf_tensor("ONESRB", [1, 128], BF16))
        IDENTF = e(nc.sbuf_tensor("IDENTF", [128, 128], F32))
        IDENTB = e(nc.sbuf_tensor("IDENTB", [128, 128], BF16))
        WFT = e(nc.sbuf_tensor("WFT", [128, NDC * H], BF16))
        BFRB = e(nc.sbuf_tensor("BFRB", [1, H], BF16))
        WRM = e(nc.sbuf_tensor("WRM", [128, 64], F32))
        ONESR64 = e(nc.sbuf_tensor("ONESR64", [65, 128], F32))

        AR = e(nc.sbuf_tensor("AR", [128, 49152], F32))

        used = 229344 - nc.sbuf_bytes_remaining
        assert used <= 229344, f"SBUF overflow: {used}"

        def at(name, shape, dtype, off):
            esz = {F32: 4, BF16: 2, U16: 2, I32: 4}[dtype]
            nbytes = shape[1] * esz
            assert off % 4 == 0 and off + nbytes <= 196608, f"{name} arena {off}+{nbytes}"
            v = AR.ap()[:, off // 4:(off + nbytes) // 4]
            if dtype != F32:
                v = v.bitcast(dtype)
            if shape[0] != 128:
                v = v[0:shape[0], :]
            return v

        ATT = at("ATT", [128, NDC * QN], BF16, 0)
        XT2 = at("XT2", [128, NDC * QN], BF16, 32768)
        XT1 = at("XT1", [128, NDC * WIN], BF16, 32768)
        WPT = [at(f"WPT{i}", [128, NDC * 512], BF16, 77824 + i * 16384) for i in range(2)]
        Qg = at("Qg", [128, 4 * QN], BF16, 110592)
        Kg = at("Kg", [128, 4 * WIN], BF16, 118784)
        Vg = at("Vg", [128, NWC * 520], BF16, 130048)
        PT = [at(f"PT{i}", [128, 512], BF16, 141504 + i * 1024) for i in range(2)]
        RRB = at("RRB", [65, 512], F32, 143552)
        RBC = at("RBC", [64, 512], F32, 145600)
        OSC = [at(f"OSC{i}", [64, 512], BF16, 186368 + i * 1024) for i in range(2)]
        SELS = at("SELS", [32, 32 * 128], BF16, 148672)
        CROWB = at("CROWB", [32, QN], BF16, 156864)
        OTB = at("OTB", [128, NDC * QN], BF16, 65536)
        WOTB = [at(f"WOTB{i}", [128, NDC * 128], BF16, 98304 + i * 4096) for i in range(2)]
        XW = [at(f"XW{i}", [128, D], F32, 106496 + i * 8192) for i in range(2)]
        XLN = at("XLN", [128, D], BF16, 122880)
        SQS = at("SQS", [128, D], F32, 126976)
        X2S = at("X2S", [128, D], F32, 135168)
        OUTS = at("OUTS", [128, D], F32, 143360)
        INTER = at("INTER", [128, 16 * QN], BF16, 0)
        GW = [at(f"GW{i}", [128, 16 * 512], BF16, 65536 + i * 16384) for i in range(2)]
        UW = [at(f"UW{i}", [128, 16 * 512], BF16, 98304 + i * 16384) for i in range(2)]
        DWNW = [at(f"DWNW{i}", [128, 16 * 256], BF16, 131072 + i * 8192) for i in range(2)]
        SQT = at("SQT", [128, 512], F32, 147456)
        GF = at("GF", [128, 512], F32, 149504)
        U2T = at("U2T", [128, 512], F32, 151552)
        T1T = at("T1T", [128, 512], F32, 153600)
        YT = at("YT", [128, 512], F32, 155648)
        THT = at("THT", [128, 512], F32, 157696)
        QT = at("QT", [128, 512], F32, 159744)
        DSTG = [at(f"DSTG{i}", [128, 512], F32, 161792 + i * 2048) for i in range(2)]
        PKB = [at(f"PKB{i}", [128, 512], U16, 165888 + i * 1024) for i in range(4)]
        SCB = [at(f"SCB{i}", [128, 512], F32, 169984 + i * 2048) for i in range(4)]
        DF0B = at("DF0B", [128, 1024], U16, 178176)
        DSGN = at("DSGN", [128, 1024], U16, 180224)
        DF0 = at("DF0", [128, 1024], U16, 182272)
        DF1 = at("DF1", [128, 1024], U16, 184320)

        class Ctr:
            __slots__ = ("h", "n")
            def __init__(self, h):
                self.h = h
                self.n = 0
            def inc(self, inst, k):
                inst.then_inc(self.h, k)
                self.n += k
                return self.n

        qs = Ctr(e(nc.semaphore("qs")))
        qg = Ctr(e(nc.semaphore("qg")))
        sst = Ctr(e(nc.semaphore("sst")))
        spe = Ctr(e(nc.semaphore("spe")))
        sdv = Ctr(e(nc.semaphore("sdv")))
        sac = Ctr(e(nc.semaphore("sac")))
        spo = Ctr(e(nc.semaphore("spo")))
        block = e(nc.Block())
        emit_on = [True]
        def _mk(reg):
            return lambda f: (reg(f) if emit_on[0] else None)
        SY, VE, SC, TE, GP = (_mk(block.sync), _mk(block.vector), _mk(block.scalar),
                              _mk(block.tensor), _mk(block.gpsimd))

        @SY
        def _(q):
            q.dma_start(out=IDENTF[:], in_=identf.ap()).then_inc(qs.h, 16)
            q.dma_start(out=IDENTB[:], in_=identb.ap()).then_inc(qs.h, 16)
            q.dma_start(out=KM[:, 0:NWC], in_=km_in).then_inc(qs.h, 16)
        qs.n += 48
        t_km = qs.n

        @GP
        def _(g):
            g.dma_start(out=BFRB[:], in_=bfr_in).then_inc(qg.h, 16)
            g.dma_start(out=WFT[:], in_=wfT_in.rearrange("(c p) h -> p c h", p=128)).then_inc(qg.h, 16)
        qg.n += 32
        t_wf = qg.n

        @VE
        def _(v):
            v.memset(WRM[:], 1.0)
            v.memset(ONESF[:], 1.0)
            v.memset(ONESRF[:], 1.0)
            v.memset(ONESRB[:], 1.0)
            v.memset(ONESR64[64:65, :], 1.0)
            v.memset(TRIF[:], 1.0)
            v.memset(SELS[:], 1.0)
            inst = v.memset(STAT[:], 0.0)
            sdv.inc(inst, 1)
        t_const = sdv.n

        @GP
        def _(g):
            g.wait_ge(sdv.h, t_const)
            g.affine_select(out=TRIF[:], in_=TRIF[:], pattern=[[1, 128]],
                            compare_op=ALU.is_ge, fill=0.0, base=0, channel_multiplier=-1)
            for h in range(H):
                inst = g.affine_select(out=SELS[:, h * 128:(h + 1) * 128],
                                       in_=SELS[:, h * 128:(h + 1) * 128],
                                       pattern=[[0, 128]], compare_op=ALU.is_equal,
                                       fill=0.0, base=-h, channel_multiplier=1)
            spo.inc(inst, 1)
        t_masks = spo.n

        @SC
        def _(s):
            s.wait_ge(sdv.h, t_const)
            s.activation(out=WRM[:], in_=WRM[:], func=AF.Exp, bias=f32(0.0), scale=f32(0.1))
            s.activation(out=WRM[:], in_=WRM[:], func=AF.Sqrt, bias=f32(1.0), scale=f32(1.0))
            s.activation(out=WRM[:], in_=WRM[:], func=AF.Tanh, bias=f32(0.0), scale=f32(1.0))
            s.activation(out=WRM[:], in_=WRM[:], func=AF.Relu, bias=f32(1.0), scale=f32(-1.0))
            s.activation(out=WRM[:], in_=WRM[:], func=AF.Square, bias=f32(0.0), scale=f32(0.1))
            inst = s.activation(out=WRM[:], in_=WRM[:], func=AF.Exp, bias=f32(0.0), scale=f32(0.1))
            sac.inc(inst, 1)

        PS0 = e(nc.psum_tensor("PS0", [128, 512], F32))
        PS1 = e(nc.psum_tensor("PS1", [128, 512], F32))
        PS2 = e(nc.psum_tensor("PS2", [128, 512], F32))
        PS3 = e(nc.psum_tensor("PS3", [128, 512], F32))
        PS4 = e(nc.psum_tensor("PS4", [128, 512], F32))
        PS5 = e(nc.psum_tensor("PS5", [128, 512], F32))
        PS6 = e(nc.psum_tensor("PS6", [128, 512], F32))
        PSS = e(nc.psum_tensor("PSS", [128, 64], F32))
        PBT = [PS4.ap().bitcast(BF16)[:, 0:128], PS5.ap().bitcast(BF16)[:, 0:128]]
        PFT = [PS2.ap()[:, 0:128], PS3.ap()[:, 0:128]]

        def ln_apply(v, src):
            return v.tensor_scalar(out=XLN[:], in0=src[:], scalar1=STAT[:, 2:3],
                                   scalar2=STAT[:, 5:6], op0=ALU.subtract, op1=ALU.mult)

        def emit_ln_chain(stats_waits, apply_waits, src):
            @VE
            def _(v):
                for h, n in stats_waits:
                    v.wait_ge(h, n)
                v.reduce_sum(STAT[:, 0:1], src[:], axis=AXX)
                v.reduce_sum(STAT[:, 1:2], SQS[:], axis=AXX)
                v.tensor_single_scalar(out=STAT[:, 2:3], in_=STAT[:, 0:1], scalar=1.0 / D, op=ALU.mult)
                inst = v.tensor_single_scalar(out=STAT[:, 3:4], in_=STAT[:, 1:2], scalar=1.0 / D, op=ALU.mult)
                sdv.inc(inst, 1)
            t_st = sdv.n
            @SC
            def _(s, t_st=t_st):
                s.wait_ge(sdv.h, t_st)
                inst = s.activation(out=STAT[:, 4:5], in_=STAT[:, 2:3], func=AF.Square,
                                    bias=f32(0.0), scale=f32(1.0))
                sac.inc(inst, 1)
            t_msq = sac.n
            @VE
            def _(v, t_msq=t_msq):
                v.wait_ge(sac.h, t_msq)
                inst = v.tensor_tensor(out=STAT[:, 4:5], in0=STAT[:, 3:4], in1=STAT[:, 4:5],
                                       op=ALU.subtract)
                sdv.inc(inst, 1)
            t_var = sdv.n
            @SC
            def _(s, t_var=t_var):
                s.wait_ge(sdv.h, t_var)
                inst = s.activation(out=STAT[:, 6:7], in_=STAT[:, 4:5], func=AF.Sqrt,
                                    bias=f32(1e-5), scale=f32(1.0))
                sac.inc(inst, 1)
            t_sqv = sac.n
            @VE
            def _(v, t_sqv=t_sqv):
                v.wait_ge(sac.h, t_sqv)
                inst = v.reciprocal(STAT[:, 5:6], STAT[:, 6:7])
                sdv.inc(inst, 1)
                v.wait_ge(sdv.h, sdv.n)
                for h, n in apply_waits:
                    v.wait_ge(h, n)
                inst = ln_apply(v, src)
                sdv.inc(inst, 1)
            return sdv.n

        xw_t = xw.rearrange("(c p) d -> c p d", p=128)
        xw_free = [0, 0]
        xt1_tok = [0] * NWC
        tr_done = [0] * NWC

        for wc in range(min(NWC, p1n)):
            buf = wc % 2
            @SY
            def _(q, wc=wc, buf=buf):
                if wc >= 2:
                    q.wait_ge(sdv.h, xw_free[buf])
                q.dma_start(out=XW[buf][:], in_=xw_t[wc]).then_inc(qs.h, 16)
            qs.n += 16
            t_x = qs.n

            @SC
            def _(s, wc=wc, buf=buf, t_x=t_x):
                s.wait_ge(qs.h, t_x)
                if wc >= 1:
                    s.wait_ge(sdv.h, xt1_tok[wc - 1])
                inst = s.activation(out=SQS[:], in_=XW[buf][:], func=AF.Square,
                                    bias=f32(0.0), scale=f32(1.0))
                sac.inc(inst, 1)
            t_sq = sac.n

            t_xln = emit_ln_chain(
                stats_waits=[(qs.h, t_x), (sac.h, t_sq)],
                apply_waits=([(spe.h, tr_done[wc - 1])] if wc >= 1 else []),
                src=XW[buf])
            xw_free[buf] = t_xln

            for ck in range(NDC if p1tr else 0):
                @TE
                def _(t, wc=wc, ck=ck, t_xln=t_xln):
                    t.wait_ge(sdv.h, t_xln)
                    if ck >= 2:
                        t.wait_ge(sdv.h, sdv.n - 1)
                    inst = t.transpose(PBT[ck % 2], XLN[:, ck * 128:(ck + 1) * 128], IDENTB[:])
                    spe.inc(inst, 1)
                t_tr = spe.n
                @VE
                def _(v, wc=wc, ck=ck, t_tr=t_tr):
                    v.wait_ge(spe.h, t_tr)
                    inst = v.tensor_copy(XT1[:, ck * WIN + wc * 128: ck * WIN + wc * 128 + 128],
                                         PBT[ck % 2])
                    sdv.inc(inst, 1)
            xt1_tok[wc] = sdv.n
            tr_done[wc] = spe.n
            if debug and wc == 1:
                @SC
                def _(s, buf=buf, t_xln=t_xln):
                    s.wait_ge(sdv.h, t_xln)
                    s.dma_start(out=dbg["xln2c"], in_=XLN[:]).then_inc(sst.h, 16)
                    s.dma_start(out=dbg["xw2c"], in_=XW[buf][:]).then_inc(sst.h, 16)
                    s.dma_start(out=dbg["stat2c"], in_=STAT[:]).then_inc(sst.h, 16)
                sst.n += 48
        t_allxt1 = xt1_tok[NWC - 1]

        if debug:
            @SC
            def _(s):
                s.wait_ge(sdv.h, t_allxt1)
                s.dma_start(out=dbg["xt1"], in_=XT1[:]).then_inc(sst.h, 16)
            sst.n += 16

        emit_on[0] = phases >= 2
        t_pss_free = sdv.n
        for wc in range(NWC):
            @TE
            def _(t, wc=wc, t_pss_free=t_pss_free):
                t.wait_ge(sdv.h, xt1_tok[wc])
                t.wait_ge(qg.h, t_wf)
                t.wait_ge(sdv.h, t_pss_free)
                for ck in range(NDC):
                    t.matmul(PSS[:, 0:H], XT1[:, ck * WIN + wc * 128: ck * WIN + wc * 128 + 128],
                             WFT[:, ck * H:(ck + 1) * H], start=(ck == 0), stop=False)
                inst = t.matmul(PSS[:, 0:H], ONESRB[0:1, :], BFRB[0:1, :], start=False, stop=True)
                spe.inc(inst, 1)
            t_lf = spe.n

            @VE
            def _(v, wc=wc, t_lf=t_lf):
                v.wait_ge(spe.h, t_lf)
                inst = v.tensor_single_scalar(out=LOGF[:, wc * H:(wc + 1) * H], in_=PSS[:, 0:H],
                                              scalar=0.0, op=ALU.min)
                sdv.inc(inst, 1)

            @SC
            def _(s, wc=wc, t_lf=t_lf):
                s.wait_ge(spe.h, t_lf)
                s.activation(out=LTMP[:], in_=PSS[:, 0:H], func=AF.Exp, bias=f32(0.0), scale=f32(1.0))
                inst = s.activation(out=LTM2[:], in_=PSS[:, 0:H], func=AF.Exp, bias=f32(0.0), scale=f32(-1.0))
                sac.inc(inst, 1)
            t_ee = sac.n

            @VE
            def _(v, wc=wc, t_ee=t_ee):
                v.wait_ge(sac.h, t_ee)
                inst = v.tensor_tensor(out=UALL[:, wc * H:(wc + 1) * H],
                                       in0=LTMP[:], in1=LTM2[:], op=ALU.min)
                sdv.inc(inst, 1)
            t_pss_free = sdv.n

        t_u_all = sdv.n
        @VE
        def _(v):
            v.wait_ge(sdv.h, t_u_all)
            v.tensor_scalar(out=HORN[:], in0=UALL[:], scalar1=LN1P_C[8],
                            scalar2=LN1P_C[7], op0=ALU.mult, op1=ALU.add)
            for k in range(6, -1, -1):
                v.tensor_tensor(out=HORN[:], in0=HORN[:], in1=UALL[:], op=ALU.mult)
                v.tensor_single_scalar(out=HORN[:], in_=HORN[:], scalar=LN1P_C[k], op=ALU.add)
            v.tensor_tensor(out=HORN[:], in0=HORN[:], in1=UALL[:], op=ALU.mult)
            inst = v.tensor_tensor(out=LOGF[:], in0=LOGF[:], in1=HORN[:], op=ALU.subtract)
            sdv.inc(inst, 1)
        t_logf = sdv.n

        t_pssr = t_logf
        for wc in range(NWC):
            @TE
            def _(t, wc=wc, t_pssr=t_pssr):
                t.wait_ge(sdv.h, t_logf)
                t.wait_ge(sdv.h, t_pssr)
                for j in range(wc + 1):
                    lhsT = ONESF if j < wc else TRIF
                    inst = t.matmul(PSS[:, 0:H], lhsT[:],
                                    LOGF[:, j * H:(j + 1) * H], start=(j == 0), stop=(j == wc))
                spe.inc(inst, 1)
            t_c = spe.n

            @VE
            def _(v, wc=wc, t_c=t_c):
                v.wait_ge(spe.h, t_c)
                v.wait_ge(spo.h, t_masks)
                v.wait_ge(qs.h, t_km)
                v.tensor_copy(CCOL[:, wc * H:(wc + 1) * H], PSS[:, 0:H])
                inst = v.tensor_scalar(out=NEGC[:, wc * H:(wc + 1) * H],
                                       in0=PSS[:, 0:H],
                                       scalar1=KM[:, wc:wc + 1], scalar2=-1.0,
                                       op0=ALU.subtract, op1=ALU.mult)
                sdv.inc(inst, 1)
            t_pssr = sdv.n
        t_negc = sdv.n

        for qc in range(NQC):
            @TE
            def _(t, qc=qc):
                t.wait_ge(sdv.h, t_negc)
                if qc >= 2:
                    t.wait_ge(sdv.h, sdv.n - 1)
                inst = t.transpose(PFT[qc % 2][0:32, :], CCOL[:, (3 + qc) * H:(4 + qc) * H],
                                   IDENTF[:])
                spe.inc(inst, 1)
            t_tr = spe.n
            @VE
            def _(v, qc=qc, t_tr=t_tr):
                v.wait_ge(spe.h, t_tr)
                inst = v.tensor_copy(CROWB[0:32, qc * 128:(qc + 1) * 128], PFT[qc % 2][0:32, :])
                sdv.inc(inst, 1)
        t_crow = sdv.n

        emit_on[0] = phases >= 3
        KTIL = [(0, 512), (512, 512), (1024, 384)]
        wpt_free = [0, 0]

        @VE
        def _(v):
            for wc in range(NWC):
                for lh in range(8):
                    v.memset(Vg[:, wc * 520 + lh * 65 + 64: wc * 520 + lh * 65 + 65], 1.0)
            inst = v.memset(STAT[:, 15:16], 0.0)
            sdv.inc(inst, 1)
        t_vones = sdv.n

        psA_free = [0, 0, 0]
        oscr_qg = [0, 0]
        ps01_free = [0, 0]
        for gi in range(4):
            @GP
            def _(g, gi=gi):
                g.wait_ge(sdv.h, wpt_free[0])
                inst = g.dma_start(
                    out=WPT[0].rearrange("p (c n) -> p c n", c=NDC),
                    in_=wqT.rearrange("(c p) n -> p c n", p=128)[:, :, gi * 512:(gi + 1) * 512])
                inst.then_inc(qg.h, 16)
            qg.n += 16
            tok_q = qg.n
            for k in range(8):
                pr, ti = k // 2, k % 2
                ps = PS4 if k % 2 == 0 else PS5
                @TE
                def _(t, gi=gi, pr=pr, ti=ti, ps=ps, k=k, tok_q=tok_q, sg=sdv.n):
                    t.wait_ge(qg.h, tok_q)
                    t.wait_ge(sdv.h, t_allxt1)
                    if k >= 2:
                        t.wait_ge(sdv.h, sdv.n - 1)
                    else:
                        t.wait_ge(sdv.h, sg)
                    for ck in range(NDC):
                        inst = t.matmul(
                            ps[:], WPT[0][:, ck * 512 + pr * 128: ck * 512 + (pr + 1) * 128],
                            XT1[:, ck * WIN + NPAD + ti * 512: ck * WIN + NPAD + (ti + 1) * 512],
                            start=(ck == 0), stop=(ck == NDC - 1))
                    spe.inc(inst, 1)
                t_mm = spe.n
                @VE
                def _(v, pr=pr, ti=ti, ps=ps, t_mm=t_mm):
                    v.wait_ge(spe.h, t_mm)
                    inst = v.tensor_single_scalar(
                        out=Qg[:, pr * QN + ti * 512: pr * QN + (ti + 1) * 512],
                        in_=ps[:], scalar=0.125, op=ALU.mult)
                    sdv.inc(inst, 1)
            wpt_free[0] = sdv.n

            @GP
            def _(g, gi=gi):
                g.wait_ge(sdv.h, wpt_free[1])
                if gi == 0:
                    g.wait_ge(sdv.h, t_allxt1)
                inst = g.dma_start(
                    out=WPT[1].rearrange("p (c n) -> p c n", c=NDC),
                    in_=wkT.rearrange("(c p) n -> p c n", p=128)[:, :, gi * 512:(gi + 1) * 512])
                inst.then_inc(qg.h, 16)
            qg.n += 16
            tok_k = qg.n
            k = 0
            for pr in range(4):
                for (w0, wl) in KTIL:
                    ps = PS4 if k % 2 == 0 else PS5
                    @TE
                    def _(t, pr=pr, w0=w0, wl=wl, ps=ps, k=k, tok_k=tok_k, sg=sdv.n):
                        t.wait_ge(qg.h, tok_k)
                        if k >= 2:
                            t.wait_ge(sdv.h, sdv.n - 1)
                        else:
                            t.wait_ge(sdv.h, sg)
                        for ck in range(NDC):
                            inst = t.matmul(
                                ps[:, 0:wl],
                                WPT[1][:, ck * 512 + pr * 128: ck * 512 + (pr + 1) * 128],
                                XT1[:, ck * WIN + w0: ck * WIN + w0 + wl],
                                start=(ck == 0), stop=(ck == NDC - 1))
                        spe.inc(inst, 1)
                    t_mm = spe.n
                    @VE
                    def _(v, pr=pr, w0=w0, wl=wl, ps=ps, t_mm=t_mm):
                        v.wait_ge(spe.h, t_mm)
                        inst = v.tensor_copy(Kg[:, pr * WIN + w0: pr * WIN + w0 + wl], ps[:, 0:wl])
                        sdv.inc(inst, 1)
                    k += 1
            wpt_free[1] = sdv.n

            @GP
            def _(g, gi=gi):
                g.wait_ge(sdv.h, wpt_free[0])
                inst = g.dma_start(
                    out=WPT[0].rearrange("p (c n) -> p c n", c=NDC),
                    in_=wvT.rearrange("(c p) n -> p c n", p=128)[:, :, gi * 512:(gi + 1) * 512])
                inst.then_inc(qg.h, 16)
            qg.n += 16
            tok_v = qg.n
            for wc in range(NWC):
                ps = PS4 if wc % 2 == 0 else PS5
                @TE
                def _(t, wc=wc, ps=ps, tok_v=tok_v, sg=sdv.n):
                    t.wait_ge(qg.h, tok_v)
                    t.wait_ge(sdv.h, t_vones)
                    if wc >= 2:
                        t.wait_ge(sdv.h, sdv.n - 1)
                    else:
                        t.wait_ge(sdv.h, sg)
                    for ck in range(NDC):
                        inst = t.matmul(
                            ps[:], XT1[:, ck * WIN + wc * 128: ck * WIN + (wc + 1) * 128],
                            WPT[0][:, ck * 512:(ck + 1) * 512],
                            start=(ck == 0), stop=(ck == NDC - 1))
                    spe.inc(inst, 1)
                t_mm = spe.n
                @VE
                def _(v, wc=wc, ps=ps, t_mm=t_mm):
                    v.wait_ge(spe.h, t_mm)
                    inst = v.tensor_copy(
                        Vg.rearrange("p (c f) -> p c f", c=NWC)[:, wc].rearrange(
                            "p (h d) -> p h d", h=8)[:, :, 0:64],
                        ps.ap().rearrange("p (h d) -> p h d", h=8))
                    sdv.inc(inst, 1)
            wpt_free[0] = sdv.n
            t_kv = sdv.n

            for lh in range(8):
                gh = gi * 8 + lh
                pr, side = lh // 2, lh % 2
                r0 = side * 64
                for ti in range(2):
                    jcs = list(range(4 * ti, 4 * ti + 7))
                    psAb = [PS2, PS3, PS6][(lh * 2 + ti) % 3]
                    psAi = (lh * 2 + ti) % 3
                    psA = psAb.ap()[0:65, :]
                    exp_tok = [0] * len(jcs)
                    for idx, jc in enumerate(jcs):
                        sb = PS0 if idx % 2 == 0 else PS1
                        @TE
                        def _(t, gh=gh, pr=pr, r0=r0, ti=ti, jc=jc, sb=sb, idx=idx,
                              psAi=psAi,
                              pf0=ps01_free[0], pf1=ps01_free[1]):
                            t.wait_ge(sdv.h, t_kv)
                            t.wait_ge(sdv.h, t_crow)
                            if idx == 0:
                                t.wait_ge(sdv.h, psA_free[psAi])
                            t.wait_ge(sac.h, (pf0 if idx % 2 == 0 else pf1))
                            t.matmul(
                                sb[:],
                                Kg[r0:r0 + 64, pr * WIN + jc * 128: pr * WIN + (jc + 1) * 128],
                                Qg[r0:r0 + 64, pr * QN + ti * 512: pr * QN + (ti + 1) * 512],
                                start=True, stop=False)
                            inst = t.matmul(
                                sb[:], SELS[:, gh * 128:(gh + 1) * 128],
                                CROWB[0:32, ti * 512:(ti + 1) * 512],
                                start=False, stop=True)
                            spe.inc(inst, 1)
                        t_s = spe.n
                        pt = PT[idx % 2]
                        @SC
                        def _(s, sb=sb, pt=pt, gh=gh, ti=ti, jc=jc, t_s=t_s):
                            s.wait_ge(spe.h, t_s)
                            inst = s.activation(
                                out=pt[:], in_=sb[:], func=AF.Exp,
                                bias=NEGC[:, jc * H + gh: jc * H + gh + 1],
                                scale=f32(1.0))
                            sac.inc(inst, 1)
                        exp_tok[idx] = sac.n
                        ps01_free[idx % 2] = sac.n
                        mbase = NPAD + 512 * ti - 128 * jc
                        if mbase < 127:
                            @GP
                            def _(g, pt=pt, mbase=mbase, tok=sac.n):
                                g.wait_ge(sac.h, tok)
                                inst = g.affine_select(
                                    out=pt[:], in_=pt[:], pattern=[[1, 512]],
                                    compare_op=ALU.is_ge, fill=0.0, base=mbase,
                                    channel_multiplier=-1)
                                spo.inc(inst, 1)
                            av_sem, av_tok = spo, spo.n
                        else:
                            av_sem, av_tok = sac, sac.n
                        @TE
                        def _(t, jc=jc, pt=pt, psA=psA, lh=lh, idx=idx,
                              sh=av_sem.h, tok=av_tok, n=len(jcs)):
                            t.wait_ge(sh, tok)
                            inst = t.matmul(psA,
                                            Vg[:, jc * 520 + lh * 65: jc * 520 + lh * 65 + 65],
                                            pt[:], start=(idx == 0), stop=(idx == n - 1))
                            spe.inc(inst, 1)
                    t_av = spe.n
                    @VE
                    def _(v, psAb=psAb, t_av=t_av):
                        v.wait_ge(spe.h, t_av)
                        inst = v.reciprocal(RRB[64:65, :], psAb.ap()[64:65, :])
                        sdv.inc(inst, 1)
                    t_rr = sdv.n
                    @TE
                    def _(t, t_rr=t_rr):
                        t.wait_ge(sdv.h, t_rr)
                        inst = t.matmul(PS4.ap()[0:64, :], ONESR64[64:65, 0:64], RRB[64:65, :],
                                        start=True, stop=True)
                        spe.inc(inst, 1)
                    t_rb = spe.n
                    @VE
                    def _(v, t_rb=t_rb):
                        v.wait_ge(spe.h, t_rb)
                        inst = v.tensor_copy(RBC[:], PS4.ap()[0:64, :])
                        sdv.inc(inst, 1)
                    if side == 0:
                        @VE
                        def _(v, psAb=psAb, gh=gh, ti=ti):
                            inst = v.tensor_tensor(
                                out=ATT[0:64, (gh // 2) * QN + ti * 512: (gh // 2) * QN + (ti + 1) * 512],
                                in0=psAb.ap()[0:64, :], in1=RBC[:], op=ALU.mult)
                            sdv.inc(inst, 1)
                        psA_free[psAi] = sdv.n
                    else:
                        oi = ti
                        @VE
                        def _(v, psAb=psAb, oi=oi, t_dma=oscr_qg[oi]):
                            v.wait_ge(qg.h, t_dma)
                            inst = v.tensor_tensor(out=OSC[oi][:], in0=psAb.ap()[0:64, :],
                                                   in1=RBC[:], op=ALU.mult)
                            sdv.inc(inst, 1)
                        psA_free[psAi] = sdv.n
                        t_os = sdv.n
                        @GP
                        def _(g, oi=oi, gh=gh, ti=ti, t_os=t_os):
                            g.wait_ge(sdv.h, t_os)
                            g.dma_start(
                                out=ATT[64:128, (gh // 2) * QN + ti * 512: (gh // 2) * QN + (ti + 1) * 512],
                                in_=OSC[oi][:]).then_inc(qg.h, 16)
                        qg.n += 16
                        oscr_qg[oi] = qg.n
        t_att_v = sdv.n
        t_att_g = qg.n

        if debug:
            @SC
            def _(s):
                s.wait_ge(sdv.h, t_att_v)
                s.wait_ge(qg.h, t_att_g)
                s.dma_start(out=dbg["logf"], in_=LOGF[:]).then_inc(sst.h, 16)
                s.dma_start(out=dbg["ccol"], in_=CCOL[:]).then_inc(sst.h, 16)
                s.dma_start(out=dbg["att"], in_=ATT[:]).then_inc(sst.h, 16)
            sst.n += 48

        emit_on[0] = phases >= 4
        wot_free = [0, 0]
        for dc in range(NDC if p3o else 0):
            slot = dc % 2
            @GP
            def _(g, dc=dc, slot=slot):
                g.wait_ge(sdv.h, wot_free[slot])
                inst = g.dma_start(
                    out=WOTB[slot].rearrange("p (c n) -> p c n", c=NDC),
                    in_=woT.rearrange("(c p) n -> p c n", p=128)[:, :, dc * 128:(dc + 1) * 128])
                inst.then_inc(qg.h, 16)
            qg.n += 16
            tok_w = qg.n
            for ti in range(2):
                ps = PS0 if ti == 0 else PS1
                @TE
                def _(t, dc=dc, ti=ti, ps=ps, slot=slot, tok_w=tok_w, ag=sac.n):
                    t.wait_ge(qg.h, tok_w)
                    t.wait_ge(sdv.h, t_att_v)
                    t.wait_ge(qg.h, t_att_g)
                    t.wait_ge(sac.h, ag)
                    if dc + ti > 0:
                        t.wait_ge(sdv.h, sdv.n - 1)
                    for ck in range(NDC):
                        inst = t.matmul(ps[:],
                                        WOTB[slot][:, ck * 128:(ck + 1) * 128],
                                        ATT[:, ck * QN + ti * 512: ck * QN + ti * 512 + 512],
                                        start=(ck == 0), stop=(ck == NDC - 1))
                    spe.inc(inst, 1)
                t_o = spe.n
                @VE
                def _(v, dc=dc, ti=ti, ps=ps, t_o=t_o):
                    v.wait_ge(spe.h, t_o)
                    inst = v.tensor_copy(OTB[:, dc * QN + ti * 512: dc * QN + (ti + 1) * 512], ps[:])
                    sdv.inc(inst, 1)
            wot_free[slot] = sdv.n
        t_otb = sdv.n

        xt2_tok = [0] * NQC
        t_prev_x2 = 0
        out2i_qg = 0
        for qc in range(min(NQC, p3n)):
            wc = qc + 3
            buf = qc % 2
            @SY
            def _(q, qc=qc, wc=wc, buf=buf):
                q.wait_ge(sdv.h, xw_free[buf])
                if qc < 2:
                    q.wait_ge(sdv.h, t_att_v)
                q.dma_start(out=XW[buf][:], in_=xw_t[wc]).then_inc(qs.h, 16)
            qs.n += 16
            t_x = qs.n

            for ck in range(NDC):
                @TE
                def _(t, qc=qc, ck=ck, t_prev_x2=t_prev_x2):
                    t.wait_ge(sdv.h, t_otb)
                    t.wait_ge(sdv.h, t_prev_x2)
                    if ck >= 2:
                        t.wait_ge(sdv.h, sdv.n - 1)
                    inst = t.transpose(PBT[ck % 2],
                                       OTB[:, ck * QN + qc * 128: ck * QN + qc * 128 + 128],
                                       IDENTB[:])
                    spe.inc(inst, 1)
                t_tr = spe.n
                @VE
                def _(v, qc=qc, ck=ck, buf=buf, t_tr=t_tr, t_x=t_x):
                    v.wait_ge(spe.h, t_tr)
                    v.wait_ge(qs.h, t_x)
                    inst = v.tensor_tensor(
                        out=X2S[:, ck * 128:(ck + 1) * 128],
                        in0=XW[buf][:, ck * 128:(ck + 1) * 128],
                        in1=PBT[ck % 2],
                        op=ALU.add)
                    sdv.inc(inst, 1)
            xw_free[buf] = sdv.n
            t_x2 = sdv.n

            if debug:
                @SC
                def _(s, qc=qc, t_x2=t_x2):
                    s.wait_ge(sdv.h, t_x2)
                    s.dma_start(out=dbg["x2"].rearrange("(c p) d -> c p d", p=128)[qc],
                                in_=X2S[:]).then_inc(sst.h, 16)
                sst.n += 16

            @SC
            def _(s, qc=qc, t_x2=t_x2):
                s.wait_ge(sdv.h, t_x2)
                inst = s.activation(out=SQS[:], in_=X2S[:], func=AF.Square, bias=f32(0.0), scale=f32(1.0))
                sac.inc(inst, 1)
            t_sq = sac.n
            t_xln2 = emit_ln_chain(
                stats_waits=[(sac.h, t_sq)],
                apply_waits=([(spe.h, spe.n)] if qc >= 1 else []),
                src=X2S)

            for gch in range(NDC):
                ck, nib = gch // 2, gch % 2
                @TE
                def _(t, qc=qc, gch=gch, ck=ck, nib=nib, t_xln2=t_xln2):
                    t.wait_ge(sdv.h, t_xln2)
                    if gch >= 2:
                        t.wait_ge(sdv.h, sdv.n - 1)
                    inst = t.transpose(PBT[gch % 2],
                                       XLN[:, 256 * ck + nib: 256 * ck + nib + 255: 2],
                                       IDENTB[:])
                    spe.inc(inst, 1)
                t_tr = spe.n
                @VE
                def _(v, qc=qc, gch=gch, t_tr=t_tr):
                    v.wait_ge(spe.h, t_tr)
                    inst = v.tensor_copy(XT2[:, gch * QN + qc * 128: gch * QN + qc * 128 + 128],
                                         PBT[gch % 2])
                    sdv.inc(inst, 1)
            xt2_tok[qc] = sdv.n

            for ck in range(NDC):
                @TE
                def _(t, qc=qc, ck=ck):
                    t.wait_ge(sdv.h, xt2_tok[qc])
                    if ck >= 2:
                        t.wait_ge(sdv.h, sdv.n - 1)
                    inst = t.transpose(PFT[ck % 2], X2S[:, ck * 128:(ck + 1) * 128], IDENTF[:])
                    spe.inc(inst, 1)
                t_tr = spe.n
                @VE
                def _(v, ck=ck, t_tr=t_tr, t_o2p=out2i_qg):
                    v.wait_ge(spe.h, t_tr)
                    if ck == 0:
                        v.wait_ge(qg.h, t_o2p)
                    inst = v.tensor_copy(OUTS[:, ck * 128:(ck + 1) * 128], PFT[ck % 2])
                    sdv.inc(inst, 1)
            t_prev_x2 = sdv.n
            t_o2i = sdv.n
            @GP
            def _(g, qc=qc, t_o2i=t_o2i):
                g.wait_ge(sdv.h, t_o2i)
                inst = g.dma_start(
                    out=out2d.rearrange("(c p) w -> p c w", p=128)[:, :, qc * 128:(qc + 1) * 128],
                    in_=OUTS.rearrange("p (c w) -> p c w", c=NDC))
                inst.then_inc(qg.h, 16)
            qg.n += 16
            out2i_qg = qg.n
        t_out2init = qg.n
        t_xt2_all = xt2_tok[NQC - 1]

        if debug:
            @SC
            def _(s):
                s.wait_ge(sdv.h, t_xt2_all)
                s.dma_start(out=dbg["xt2"], in_=XT2[:]).then_inc(sst.h, 16)
            sst.n += 16

        emit_on[0] = phases >= 5
        pk_free = [0, 0, 0, 0]
        slotc = [0]
        dec_par = [0]
        ffn_pre = [True]

        def decode_chunk(pk_ap, sc_ap, dst_both, wd):
            A, B, AB = slice(0, wd), slice(wd, 2 * wd), slice(0, 2 * wd)
            par = dec_par[0] % 2
            dec_par[0] += 1
            DF = DF0 if par == 0 else DF0B
            @VE
            def _(v, DF=DF):
                v.tensor_scalar(out=DF[:, A], in0=pk_ap, scalar1=7, scalar2=6,
                                op0=ALU.bitwise_and, op1=ALU.logical_shift_left)
                v.tensor_scalar(out=DF[:, B], in0=pk_ap, scalar1=0x70, scalar2=2,
                                op0=ALU.bitwise_and, op1=ALU.logical_shift_left)
                v.tensor_scalar(out=DSGN[:, A], in0=pk_ap, scalar1=8, scalar2=12,
                                op0=ALU.bitwise_and, op1=ALU.logical_shift_left)
                v.tensor_scalar(out=DSGN[:, B], in0=pk_ap, scalar1=128, scalar2=8,
                                op0=ALU.bitwise_and, op1=ALU.logical_shift_left)
                v.tensor_single_scalar(out=DF[:, AB], in_=DF[:, AB], scalar=0x3F00, op=ALU.add)
                v.tensor_scalar(out=DF1.bitcast(BF16)[:, AB], in0=DF.bitcast(BF16)[:, AB],
                                scalar1=2.0, scalar2=-1.0, op0=ALU.mult, op1=ALU.add)
                v.tensor_tensor(out=DF.bitcast(BF16)[:, AB], in0=DF.bitcast(BF16)[:, AB],
                                in1=DF1.bitcast(BF16)[:, AB], op=ALU.min)
                v.tensor_tensor(out=DF[:, AB], in0=DF[:, AB], in1=DSGN[:, AB],
                                op=ALU.bitwise_or)
                v.tensor_tensor(out=dst_both[:, 0:wd], in0=DF.bitcast(BF16)[:, A],
                                in1=sc_ap, op=ALU.mult)
                inst = v.tensor_tensor(out=dst_both[:, wd:2 * wd], in0=DF.bitcast(BF16)[:, B],
                                       in1=sc_ap, op=ALU.mult)
                sdv.inc(inst, 1)
            return sdv.n

        gw_free = [0, 0]
        uw_free = [0, 0]
        gw_tok = [0, 0]
        uw_tok = [0, 0]
        inter_last = 0
        dwn_free = [0, 0]

        def load_decode_window(which, w, buf):
            src_pk = gpk if which == "g" else upk
            src_sc = gsc if which == "g" else usc
            dstT = (GW if which == "g" else UW)[buf]
            freetok = (gw_free if which == "g" else uw_free)[buf]
            last = 0
            for ck in range(8):
                slot = slotc[0] % 4
                slotc[0] += 1
                @SY
                def _(q, w=w, ck=ck, slot=slot, freetok=freetok, src_pk=src_pk, src_sc=src_sc):
                    q.wait_ge(sdv.h, pk_free[slot])
                    q.wait_ge(spe.h, freetok)
                    q.dma_start(out=PKB[slot][:],
                                in_=src_pk[w * 512:(w + 1) * 512, ck * 128:(ck + 1) * 128],
                                transpose=True).then_inc(qs.h, 16)
                    q.dma_start(out=SCB[slot][:],
                                in_=src_sc[ck * 128:(ck + 1) * 128, w * 512:(w + 1) * 512]
                                ).then_inc(qs.h, 16)
                qs.n += 32
                t_pk = qs.n
                @VE
                def _(v, t_pk=t_pk):
                    v.wait_ge(qs.h, t_pk)
                last = decode_chunk(PKB[slot][:], SCB[slot][:],
                                    dstT[:, (2 * ck) * 512:(2 * ck + 2) * 512], 512)
                pk_free[slot] = last
            if which == "g":
                gw_tok[buf] = last
            else:
                uw_tok[buf] = last

        psGU_free = [0, 0, 0, 0]
        for s_ in range(NSUP):
            for wv in range(4):
                w = s_ * 4 + wv
                buf = w % 2
                load_decode_window("g", w, buf)
                load_decode_window("u", w, buf)
                for nch in range(4):
                    n_in_sup = wv * 4 + nch
                    for ti in range(2):
                        par = (nch * 2 + ti) % 2
                        psG = PS0 if par == 0 else PS1
                        psU = PS2 if par == 0 else PS3
                        @TE
                        def _(t, w=w, buf=buf, nch=nch, ti=ti, psG=psG, psU=psU, par=par, sg=sdv.n):
                            t.wait_ge(sdv.h, t_xt2_all)
                            if w == 0 and nch == 0 and ti == 0:
                                t.wait_ge(sdv.h, sg)
                            t.wait_ge(sdv.h, gw_tok[buf])
                            t.wait_ge(sdv.h, uw_tok[buf])
                            t.wait_ge(sdv.h, psGU_free[par])
                            t.wait_ge(sdv.h, psGU_free[2 + par])
                            for g16 in range(16):
                                inst = t.matmul(
                                    psG[:], GW[buf][:, g16 * 512 + nch * 128: g16 * 512 + (nch + 1) * 128],
                                    XT2[:, g16 * QN + ti * 512: g16 * QN + (ti + 1) * 512],
                                    start=(g16 == 0), stop=(g16 == 15))
                            spe.inc(inst, 1)
                            for g16 in range(16):
                                inst = t.matmul(
                                    psU[:], UW[buf][:, g16 * 512 + nch * 128: g16 * 512 + (nch + 1) * 128],
                                    XT2[:, g16 * QN + ti * 512: g16 * QN + (ti + 1) * 512],
                                    start=(g16 == 0), stop=(g16 == 15))
                            spe.inc(inst, 1)
                        t_gu = spe.n
                        @VE
                        def _(v, psG=psG, t_gu=t_gu):
                            v.wait_ge(spe.h, t_gu)
                            v.tensor_copy(GF[:], psG[:])
                            v.tensor_tensor(out=SQT[:], in0=GF[:], in1=GF[:], op=ALU.mult)
                            v.tensor_scalar(out=YT[:], in0=SQT[:], scalar1=float(GELU_A), scalar2=float(GELU_B),
                                            op0=ALU.mult, op1=ALU.add)
                            inst = v.tensor_tensor(out=U2T[:], in0=YT[:], in1=GF[:], op=ALU.mult)
                            sdv.inc(inst, 1)
                        t_u2 = sdv.n
                        psGU_free[par] = t_u2
                        @SC
                        def _(s, t_u2=t_u2):
                            s.wait_ge(sdv.h, t_u2)
                            inst = s.activation(out=THT[:], in_=U2T[:], func=AF.Tanh,
                                                bias=f32(0.0), scale=f32(1.0))
                            sac.inc(inst, 1)
                        t_th = sac.n
                        @VE
                        def _(v, t_th=t_th):
                            v.wait_ge(sac.h, t_th)
                            inst = v.tensor_scalar(out=QT[:], in0=THT[:], scalar1=0.5,
                                                   scalar2=0.5, op0=ALU.mult, op1=ALU.add)
                            sdv.inc(inst, 1)
                        t_qt = sdv.n
                        @GP
                        def _(g, t_qt=t_qt):
                            g.wait_ge(sdv.h, t_qt)
                            inst = g.tensor_tensor(out=T1T[:], in0=QT[:], in1=GF[:], op=ALU.mult)
                            spo.inc(inst, 1)
                        t_t1 = spo.n
                        @VE
                        def _(v, psU=psU, n_in_sup=n_in_sup, ti=ti, t_t1=t_t1):
                            v.wait_ge(spo.h, t_t1)
                            inst = v.tensor_tensor(
                                out=INTER[:, n_in_sup * QN + ti * 512: n_in_sup * QN + (ti + 1) * 512],
                                in0=T1T[:], in1=psU[:], op=ALU.mult)
                            sdv.inc(inst, 1)
                        psGU_free[2 + par] = sdv.n
                        inter_last = sdv.n
                gw_free[buf] = spe.n
                uw_free[buf] = spe.n

            psD_free = [0, 0]
            if s_ == 0:
                dstg_qg = [0, 0]
            for hw in range(8):
                dbuf = hw % 2
                dw_last = 0
                for ck in range(8):
                    slot = slotc[0] % 4
                    slotc[0] += 1
                    @SY
                    def _(q, s_=s_, hw=hw, ck=ck, slot=slot, dbuf=dbuf):
                        q.wait_ge(sdv.h, pk_free[slot])
                        q.wait_ge(spe.h, dwn_free[dbuf])
                        q.dma_start(out=PKB[slot][0:128, 0:256],
                                    in_=dpk[hw * 256:(hw + 1) * 256,
                                            s_ * 1024 + ck * 128: s_ * 1024 + (ck + 1) * 128],
                                    transpose=True).then_inc(qs.h, 16)
                        q.dma_start(out=SCB[slot][0:128, 0:256],
                                    in_=dsc[s_ * 1024 + ck * 128: s_ * 1024 + (ck + 1) * 128,
                                            hw * 256:(hw + 1) * 256]).then_inc(qs.h, 16)
                    qs.n += 32
                    t_pk = qs.n
                    @VE
                    def _(v, t_pk=t_pk):
                        v.wait_ge(qs.h, t_pk)
                    dw_last = decode_chunk(PKB[slot][0:128, 0:256], SCB[slot][0:128, 0:256],
                                           DWNW[dbuf][:, (2 * ck) * 256:(2 * ck + 2) * 256], 256)
                    pk_free[slot] = dw_last
                    if debug and s_ == NSUP - 1 and hw == 7 and ck == 7:
                        @SC
                        def _(s, slot=slot, dw_last=dw_last):
                            s.wait_ge(sdv.h, dw_last)
                            s.dma_start(out=dbg["dpkb"], in_=PKB[slot][0:128, 0:256]).then_inc(sst.h, 16)
                            s.dma_start(out=dbg["dscb"], in_=SCB[slot][0:128, 0:256]).then_inc(sst.h, 16)
                            s.dma_start(out=dbg["dwnw"], in_=DWNW[1][:]).then_inc(sst.h, 16)
                        sst.n += 48
                for hc2 in range(2):
                    hcg = hw * 2 + hc2
                    for ti in range(2):
                        par = (hc2 * 2 + ti) % 2
                        ps = PS4 if par == 0 else PS5
                        @TE
                        def _(t, hc2=hc2, ti=ti, ps=ps, dbuf=dbuf, dw_last=dw_last, par=par):
                            t.wait_ge(sdv.h, dw_last)
                            t.wait_ge(sdv.h, inter_last)
                            t.wait_ge(sdv.h, psD_free[par])
                            for kc in range(16):
                                inst = t.matmul(
                                    ps[:],
                                    DWNW[dbuf][:, kc * 256 + hc2 * 128: kc * 256 + (hc2 + 1) * 128],
                                    INTER[:, kc * QN + ti * 512: kc * QN + (ti + 1) * 512],
                                    start=(kc == 0), stop=(kc == 15))
                            spe.inc(inst, 1)
                        t_dn = spe.n
                        dstg = DSTG[par]
                        @VE
                        def _(v, ps=ps, dstg=dstg, t_dn=t_dn, t_dma=dstg_qg[par]):
                            v.wait_ge(spe.h, t_dn)
                            v.wait_ge(qg.h, t_dma)
                            inst = v.tensor_copy(dstg[:], ps[:])
                            sdv.inc(inst, 1)
                        psD_free[par] = sdv.n
                        t_ds = sdv.n
                        @GP
                        def _(g, s_=s_, hcg=hcg, ti=ti, dstg=dstg, t_ds=t_ds):
                            g.wait_ge(sdv.h, t_ds)
                            if s_ == 0:
                                g.wait_ge(qg.h, t_out2init)
                            else:
                                g.wait_ge(qg.h, qg.n)
                            inst = g.dma_start(
                                out=out2d[hcg * 128:(hcg + 1) * 128, ti * 512:(ti + 1) * 512],
                                in_=dstg[:], accum_op=ALU.add)
                            inst.then_inc(qg.h, 16)
                        qg.n += 16
                        dstg_qg[par] = qg.n
                dwn_free[dbuf] = spe.n

        t_ffn_done = qg.n

        if debug:
            @SC
            def _(s):
                s.wait_ge(sdv.h, inter_last)
                s.dma_start(out=dbg["inter"], in_=INTER[:, 0:4 * QN]).then_inc(sst.h, 16)
                s.dma_start(out=dbg["gw"], in_=GW[0][:]).then_inc(sst.h, 16)
                s.wait_ge(qg.h, t_ffn_done)
                s.dma_start(out=dbg["out2d"], in_=out2d).then_inc(sst.h, 16)
                s.dma_start(out=dbg["dstg0"], in_=DSTG[0][:]).then_inc(sst.h, 16)
                s.dma_start(out=dbg["dstg1"], in_=DSTG[1][:]).then_inc(sst.h, 16)
            sst.n += 80

        emit_on[0] = phases >= 6
        t_prev_f = 0
        for qc in range(NQC):
            @SY
            def _(q, qc=qc, t_prev_f=t_prev_f):
                q.wait_ge(qg.h, t_ffn_done)
                q.wait_ge(sdv.h, t_prev_f)
                q.dma_start(out=X2S[:],
                            in_=out2d.rearrange("(c p) w -> p c w", p=128)[:, :, qc * 128:(qc + 1) * 128]
                            ).then_inc(qs.h, 16)
            qs.n += 16
            t_l = qs.n
            for ck in range(NDC):
                @TE
                def _(t, ck=ck, t_l=t_l, sg=sdv.n):
                    t.wait_ge(qs.h, t_l)
                    if ck >= 2:
                        t.wait_ge(sdv.h, sdv.n - 1)
                    else:
                        t.wait_ge(sdv.h, sg)
                    inst = t.transpose(PFT[ck % 2], X2S[:, ck * 128:(ck + 1) * 128], IDENTF[:])
                    spe.inc(inst, 1)
                t_tr = spe.n
                @VE
                def _(v, ck=ck, t_tr=t_tr, t_stp=sst.n):
                    v.wait_ge(spe.h, t_tr)
                    if ck == 0:
                        v.wait_ge(sst.h, t_stp)
                    inst = v.tensor_copy(OUTS[:, ck * 128:(ck + 1) * 128], PFT[ck % 2])
                    sdv.inc(inst, 1)
            t_of = sdv.n
            t_prev_f = t_of
            @SC
            def _(s, qc=qc, t_of=t_of):
                s.wait_ge(sdv.h, t_of)
                s.dma_start(out=out.rearrange("(c p) d -> c p d", p=128)[qc],
                            in_=OUTS[:]).then_inc(sst.h, 16)
            sst.n += 16

        @SC
        def _(s):
            s.wait_ge(sst.h, sst.n)

    return nc


def _perm_mlp():
    idx = np.arange(MLP).reshape(64, 256)
    return np.concatenate([idx[:, 0::2], idx[:, 1::2]], axis=1).reshape(-1)


def _prep_inputs(inp):
    x = np.ascontiguousarray(np.asarray(inp["L0_attn_x"], dtype=np.float32))
    Wq = np.asarray(inp["L0_attn_Wq"], dtype=np.float32)
    Wk = np.asarray(inp["L0_attn_Wk"], dtype=np.float32)
    Wv = np.asarray(inp["L0_attn_Wv"], dtype=np.float32)
    Wf = np.asarray(inp["L0_attn_Wf"], dtype=np.float32)
    bf_ = np.asarray(inp["L0_attn_bf"], dtype=np.float32)
    Wo = np.asarray(inp["L0_attn_Wo"], dtype=np.float32)
    gW = np.asarray(inp["L0_ffn_gate_W_packed"])
    gS = np.asarray(inp["L0_ffn_gate_scales"], dtype=np.float32)
    uW = np.asarray(inp["L0_ffn_up_W_packed"])
    uS = np.asarray(inp["L0_ffn_up_scales"], dtype=np.float32)
    dW = np.asarray(inp["L0_ffn_down_W_packed"])
    dS = np.asarray(inp["L0_ffn_down_scales"], dtype=np.float32)

    perm = _perm_mlp()
    g_u16 = np.ascontiguousarray(
        np.ascontiguousarray(gW.astype('<i4')).view(np.uint16).reshape(MLP, D // 2, 2)[:, :, 0][perm])
    u_u16 = np.ascontiguousarray(
        np.ascontiguousarray(uW.astype('<i4')).view(np.uint16).reshape(MLP, D // 2, 2)[:, :, 0][perm])
    d_u16 = np.ascontiguousarray(
        np.ascontiguousarray(dW.astype('<i4')).view(np.uint16).reshape(D, MLP // 2, 2)[:, :, 0])
    gscT = np.ascontiguousarray(np.repeat(gS[perm].T, 16, axis=0))
    uscT = np.ascontiguousarray(np.repeat(uS[perm].T, 16, axis=0))
    dscT = np.ascontiguousarray(np.repeat(dS.T, 16, axis=0))

    shared = dict(
        wqT=np.ascontiguousarray(Wq.T), wkT=np.ascontiguousarray(Wk.T),
        wvT=np.ascontiguousarray(Wv.T), woT=np.ascontiguousarray(Wo.T),
        wfT=np.ascontiguousarray(Wf.T), bfr=np.ascontiguousarray(bf_.reshape(1, H)),
        gpk=g_u16, upk=u_u16, dpk=d_u16, gsc=gscT, usc=uscT, dsc=dscT)

    in_maps = []
    for c in range(8):
        b, half = c // 2, c % 2
        if half == 0:
            xwc = np.concatenate([np.zeros((NPAD, D), np.float32), x[b, 0:QN]], axis=0)
            km = np.zeros((NWC, 128), np.float32)
            km.reshape(-1)[:NPAD] = NEG
        else:
            xwc = np.ascontiguousarray(x[b, T - WIN:T])
            km = np.zeros((NWC, 128), np.float32)
        m = dict(shared)
        m["xw"] = xwc
        m["km"] = np.ascontiguousarray(km.T)
        in_maps.append(m)
    return in_maps


def kernel(**inp):
    in_maps = _prep_inputs(inp)
    nc = _build(debug=False)
    res = run_bass_kernel_spmd(nc, in_maps, core_ids=list(range(8)))
    outp = np.empty((B, T, D), np.float32)
    for c in range(8):
        b, half = c // 2, c % 2
        outp[b, half * QN:(half + 1) * QN] = res.results[c]["out"]
    return outp


# version 16
# speedup vs baseline: 1.1898x; 1.0892x over previous
import numpy as np
import ml_dtypes
import concourse.bass as bass
import concourse.mybir as mybir
from concourse.ap import AP
from concourse.bass_utils import run_bass_kernel_spmd

F32 = mybir.dt.float32
BF16 = mybir.dt.bfloat16
U16 = mybir.dt.uint16
I32 = mybir.dt.int32
ALU = mybir.AluOpType
AF = mybir.ActivationFunctionType
AXX = mybir.AxisListType.X
f32 = np.float32

B, T, D, H, DH = 4, 2048, 2048, 32, 64
WIN = 1408
QN = 1024
NPAD = 384
NWC = WIN // 128
NQC = QN // 128
NDC = D // 128
MLP = 16384
NSUP = 8
NEG = -1.0e30
GELU_A = f32(0.7978845608028654 * 0.044715)
GELU_B = f32(0.7978845608028654)
RSQRT_MAGIC = 0x5F3759DF
LN1P_C = [1.0, -0.49999502, 0.3331927, -0.24844374, 0.19111443,
          -0.13674787, 0.07836182, -0.02958858, 0.00525347]


def _build(debug=False, detect_races=True, phases=9, p1n=99, p1tr=True, p1ln=True, p3o=True, p3n=99):
    nc = bass.Bass("TRN2", target_bir_lowering=False, debug=False,
                   detect_race_conditions=detect_races)

    xw = nc.dram_tensor("xw", [WIN, D], F32, kind="ExternalInput").ap()
    km_in = nc.dram_tensor("km", [128, NWC], F32, kind="ExternalInput").ap()
    wqT = nc.dram_tensor("wqT", [D, D], F32, kind="ExternalInput").ap()
    wkT = nc.dram_tensor("wkT", [D, D], F32, kind="ExternalInput").ap()
    wvT = nc.dram_tensor("wvT", [D, D], F32, kind="ExternalInput").ap()
    woT = nc.dram_tensor("woT", [D, D], F32, kind="ExternalInput").ap()
    wfT_in = nc.dram_tensor("wfT", [D, H], F32, kind="ExternalInput").ap()
    bfr_in = nc.dram_tensor("bfr", [1, H], F32, kind="ExternalInput").ap()
    gpk = nc.dram_tensor("gpk", [MLP, D // 2], U16, kind="ExternalInput").ap()
    upk = nc.dram_tensor("upk", [MLP, D // 2], U16, kind="ExternalInput").ap()
    dpk = nc.dram_tensor("dpk", [D, MLP // 2], U16, kind="ExternalInput").ap()
    gsc = nc.dram_tensor("gsc", [D // 2, MLP], F32, kind="ExternalInput").ap()
    usc = nc.dram_tensor("usc", [D // 2, MLP], F32, kind="ExternalInput").ap()
    dsc = nc.dram_tensor("dsc", [MLP // 2, D], F32, kind="ExternalInput").ap()
    out = nc.dram_tensor("out", [QN, D], F32, kind="ExternalOutput").ap()

    out2d = nc.dram_tensor("out2d", [D, QN], F32, kind="Internal").ap()

    dbg = {}
    if debug:
        dbg["xt1"] = nc.dram_tensor("dbg_xt1", [128, NDC * WIN], BF16, kind="ExternalOutput").ap()
        dbg["logf"] = nc.dram_tensor("dbg_logf", [128, NWC * H], F32, kind="ExternalOutput").ap()
        dbg["ccol"] = nc.dram_tensor("dbg_ccol", [128, NWC * H], F32, kind="ExternalOutput").ap()
        dbg["att"] = nc.dram_tensor("dbg_att", [128, NDC * QN], BF16, kind="ExternalOutput").ap()
        dbg["x2"] = nc.dram_tensor("dbg_x2", [QN, D], F32, kind="ExternalOutput").ap()
        dbg["xt2"] = nc.dram_tensor("dbg_xt2", [128, NDC * QN], BF16, kind="ExternalOutput").ap()
        dbg["inter"] = nc.dram_tensor("dbg_inter", [128, 4 * QN], BF16, kind="ExternalOutput").ap()
        dbg["gw"] = nc.dram_tensor("dbg_gw", [128, 16 * 512], BF16, kind="ExternalOutput").ap()
        dbg["out2d"] = nc.dram_tensor("dbg_out2d", [2048, 1024], F32, kind="ExternalOutput").ap()
        dbg["dwnw"] = nc.dram_tensor("dbg_dwnw", [128, 4096], BF16, kind="ExternalOutput").ap()
        dbg["dstg0"] = nc.dram_tensor("dbg_dstg0", [128, 512], F32, kind="ExternalOutput").ap()
        dbg["dstg1"] = nc.dram_tensor("dbg_dstg1", [128, 512], F32, kind="ExternalOutput").ap()
        dbg["dpkb"] = nc.dram_tensor("dbg_dpkb", [128, 256], U16, kind="ExternalOutput").ap()
        dbg["dscb"] = nc.dram_tensor("dbg_dscb", [128, 256], F32, kind="ExternalOutput").ap()
        dbg["xln2c"] = nc.dram_tensor("dbg_xln2c", [128, D], BF16, kind="ExternalOutput").ap()
        dbg["xw2c"] = nc.dram_tensor("dbg_xw2c", [128, D], F32, kind="ExternalOutput").ap()
        dbg["stat2c"] = nc.dram_tensor("dbg_stat2c", [128, 16], F32, kind="ExternalOutput").ap()

    identf = nc.inline_tensor(np.eye(128, dtype=np.float32), "identf")
    identb = nc.inline_tensor(np.eye(128, dtype=ml_dtypes.bfloat16), "identb")

    from contextlib import ExitStack
    with ExitStack() as st_:
        e = st_.enter_context

        STAT = e(nc.sbuf_tensor("STAT", [128, 16], F32))
        KM = e(nc.sbuf_tensor("KM", [128, 16], F32))
        LOGF = e(nc.sbuf_tensor("LOGF", [128, NWC * H], F32))
        LTMP = e(nc.sbuf_tensor("LTMP", [128, H], F32))
        LTM2 = e(nc.sbuf_tensor("LTM2", [128, H], F32))
        UALL = e(nc.sbuf_tensor("UALL", [128, NWC * H], F32))
        HORN = e(nc.sbuf_tensor("HORN", [128, NWC * H], F32))
        CCOL = e(nc.sbuf_tensor("CCOL", [128, NWC * H], F32))
        NEGC = e(nc.sbuf_tensor("NEGC", [128, NWC * H], F32))
        TRIF = e(nc.sbuf_tensor("TRIF", [128, 128], F32))
        ONESF = e(nc.sbuf_tensor("ONESF", [128, 128], F32))
        ONESRF = e(nc.sbuf_tensor("ONESRF", [1, 128], F32))
        ONESRB = e(nc.sbuf_tensor("ONESRB", [1, 128], BF16))
        IDENTF = e(nc.sbuf_tensor("IDENTF", [128, 128], F32))
        IDENTB = e(nc.sbuf_tensor("IDENTB", [128, 128], BF16))
        WFT = e(nc.sbuf_tensor("WFT", [128, NDC * H], BF16))
        BFRB = e(nc.sbuf_tensor("BFRB", [1, H], BF16))
        WRM = e(nc.sbuf_tensor("WRM", [128, 64], F32))
        ONESR64 = e(nc.sbuf_tensor("ONESR64", [65, 128], F32))

        AR = e(nc.sbuf_tensor("AR", [128, 49152], F32))

        used = 229344 - nc.sbuf_bytes_remaining
        assert used <= 229344, f"SBUF overflow: {used}"

        def at(name, shape, dtype, off):
            esz = {F32: 4, BF16: 2, U16: 2, I32: 4}[dtype]
            nbytes = shape[1] * esz
            assert off % 4 == 0 and off + nbytes <= 196608, f"{name} arena {off}+{nbytes}"
            v = AR.ap()[:, off // 4:(off + nbytes) // 4]
            if dtype != F32:
                v = v.bitcast(dtype)
            if shape[0] != 128:
                v = v[0:shape[0], :]
            return v

        ATT = at("ATT", [128, NDC * QN], BF16, 0)
        XT2 = at("XT2", [128, NDC * QN], BF16, 32768)
        XT1 = at("XT1", [128, NDC * WIN], BF16, 32768)
        WPT = [at(f"WPT{i}", [128, NDC * 512], BF16, 77824 + i * 16384) for i in range(2)]
        Qg = at("Qg", [128, 4 * QN], BF16, 110592)
        Kg = at("Kg", [128, 4 * WIN], BF16, 118784)
        Vg = at("Vg", [128, NWC * 520], BF16, 130048)
        PT = [at(f"PT{i}", [128, 512], BF16, 141504 + i * 1024) for i in range(2)]
        RRB = at("RRB", [65, 512], F32, 143552)
        RBC = at("RBC", [64, 512], F32, 145600)
        OSC = [at(f"OSC{i}", [64, 512], BF16, 186368 + i * 1024) for i in range(2)]
        SCR = at("SCR", [128, 3584], F32, 158912)
        EXPO = at("EXPO", [128, 3584], BF16, 188416)
        SELS = at("SELS", [32, 32 * 128], BF16, 148672)
        CROWB = at("CROWB", [32, QN], BF16, 156864)
        OTB = at("OTB", [128, NDC * QN], BF16, 65536)
        WOTB = [at(f"WOTB{i}", [128, NDC * 128], BF16, 98304 + i * 4096) for i in range(2)]
        XW = [at(f"XW{i}", [128, D], F32, 106496 + i * 8192) for i in range(2)]
        XLN = at("XLN", [128, D], BF16, 122880)
        SQS = at("SQS", [128, D], F32, 126976)
        X2S = at("X2S", [128, D], F32, 135168)
        OUTS = at("OUTS", [128, D], F32, 143360)
        INTER = at("INTER", [128, 16 * QN], BF16, 0)
        GW = [at(f"GW{i}", [128, 16 * 512], BF16, 65536 + i * 16384) for i in range(2)]
        UW = [at(f"UW{i}", [128, 16 * 512], BF16, 98304 + i * 16384) for i in range(2)]
        DWNW = [at(f"DWNW{i}", [128, 16 * 256], BF16, 131072 + i * 8192) for i in range(2)]
        SQT = at("SQT", [128, 512], F32, 147456)
        GF = at("GF", [128, 512], F32, 149504)
        U2T = at("U2T", [128, 512], F32, 151552)
        T1T = at("T1T", [128, 512], F32, 153600)
        YT = at("YT", [128, 512], F32, 155648)
        THT = at("THT", [128, 512], F32, 157696)
        QT = at("QT", [128, 512], F32, 159744)
        DSTG = [at(f"DSTG{i}", [128, 512], F32, 161792 + i * 2048) for i in range(2)]
        PKB = [at(f"PKB{i}", [128, 512], U16, 165888 + i * 1024) for i in range(4)]
        SCB = [at(f"SCB{i}", [128, 512], F32, 169984 + i * 2048) for i in range(4)]
        DF0B = at("DF0B", [128, 1024], U16, 178176)
        DSGN = at("DSGN", [128, 1024], U16, 180224)
        DF0 = at("DF0", [128, 1024], U16, 182272)
        DF1 = at("DF1", [128, 1024], U16, 184320)

        class Ctr:
            __slots__ = ("h", "n")
            def __init__(self, h):
                self.h = h
                self.n = 0
            def inc(self, inst, k):
                inst.then_inc(self.h, k)
                self.n += k
                return self.n

        qs = Ctr(e(nc.semaphore("qs")))
        qg = Ctr(e(nc.semaphore("qg")))
        sst = Ctr(e(nc.semaphore("sst")))
        spe = Ctr(e(nc.semaphore("spe")))
        sdv = Ctr(e(nc.semaphore("sdv")))
        sac = Ctr(e(nc.semaphore("sac")))
        spo = Ctr(e(nc.semaphore("spo")))
        block = e(nc.Block())
        emit_on = [True]
        def _mk(reg):
            return lambda f: (reg(f) if emit_on[0] else None)
        SY, VE, SC, TE, GP = (_mk(block.sync), _mk(block.vector), _mk(block.scalar),
                              _mk(block.tensor), _mk(block.gpsimd))

        @SY
        def _(q):
            q.dma_start(out=IDENTF[:], in_=identf.ap()).then_inc(qs.h, 16)
            q.dma_start(out=IDENTB[:], in_=identb.ap()).then_inc(qs.h, 16)
            q.dma_start(out=KM[:, 0:NWC], in_=km_in).then_inc(qs.h, 16)
        qs.n += 48
        t_km = qs.n

        @GP
        def _(g):
            g.dma_start(out=BFRB[:], in_=bfr_in).then_inc(qg.h, 16)
            g.dma_start(out=WFT[:], in_=wfT_in.rearrange("(c p) h -> p c h", p=128)).then_inc(qg.h, 16)
        qg.n += 32
        t_wf = qg.n

        @VE
        def _(v):
            v.memset(WRM[:], 1.0)
            v.memset(ONESF[:], 1.0)
            v.memset(ONESRF[:], 1.0)
            v.memset(ONESRB[:], 1.0)
            v.memset(ONESR64[64:65, :], 1.0)
            v.memset(TRIF[:], 1.0)
            v.memset(SELS[:], 1.0)
            inst = v.memset(STAT[:], 0.0)
            sdv.inc(inst, 1)
        t_const = sdv.n

        @GP
        def _(g):
            g.wait_ge(sdv.h, t_const)
            g.affine_select(out=TRIF[:], in_=TRIF[:], pattern=[[1, 128]],
                            compare_op=ALU.is_ge, fill=0.0, base=0, channel_multiplier=-1)
            for h in range(H):
                inst = g.affine_select(out=SELS[:, h * 128:(h + 1) * 128],
                                       in_=SELS[:, h * 128:(h + 1) * 128],
                                       pattern=[[0, 128]], compare_op=ALU.is_equal,
                                       fill=0.0, base=-h, channel_multiplier=1)
            spo.inc(inst, 1)
        t_masks = spo.n

        @SC
        def _(s):
            s.wait_ge(sdv.h, t_const)
            s.activation(out=WRM[:], in_=WRM[:], func=AF.Exp, bias=f32(0.0), scale=f32(0.1))
            s.activation(out=WRM[:], in_=WRM[:], func=AF.Sqrt, bias=f32(1.0), scale=f32(1.0))
            s.activation(out=WRM[:], in_=WRM[:], func=AF.Tanh, bias=f32(0.0), scale=f32(1.0))
            s.activation(out=WRM[:], in_=WRM[:], func=AF.Relu, bias=f32(1.0), scale=f32(-1.0))
            s.activation(out=WRM[:], in_=WRM[:], func=AF.Square, bias=f32(0.0), scale=f32(0.1))
            inst = s.activation(out=WRM[:], in_=WRM[:], func=AF.Exp, bias=f32(0.0), scale=f32(0.1))
            sac.inc(inst, 1)

        PS0 = e(nc.psum_tensor("PS0", [128, 512], F32))
        PS1 = e(nc.psum_tensor("PS1", [128, 512], F32))
        PS2 = e(nc.psum_tensor("PS2", [128, 512], F32))
        PS3 = e(nc.psum_tensor("PS3", [128, 512], F32))
        PS4 = e(nc.psum_tensor("PS4", [128, 512], F32))
        PS5 = e(nc.psum_tensor("PS5", [128, 512], F32))
        PS6 = e(nc.psum_tensor("PS6", [128, 512], F32))
        PSS = e(nc.psum_tensor("PSS", [128, 64], F32))
        PBT = [PS4.ap().bitcast(BF16)[:, 0:128], PS5.ap().bitcast(BF16)[:, 0:128]]
        PFT = [PS2.ap()[:, 0:128], PS3.ap()[:, 0:128]]

        def ln_apply(v, src):
            return v.tensor_scalar(out=XLN[:], in0=src[:], scalar1=STAT[:, 2:3],
                                   scalar2=STAT[:, 5:6], op0=ALU.subtract, op1=ALU.mult)

        def emit_ln_chain(stats_waits, apply_waits, src):
            @VE
            def _(v):
                for h, n in stats_waits:
                    v.wait_ge(h, n)
                v.reduce_sum(STAT[:, 0:1], src[:], axis=AXX)
                v.reduce_sum(STAT[:, 1:2], SQS[:], axis=AXX)
                v.tensor_single_scalar(out=STAT[:, 2:3], in_=STAT[:, 0:1], scalar=1.0 / D, op=ALU.mult)
                inst = v.tensor_single_scalar(out=STAT[:, 3:4], in_=STAT[:, 1:2], scalar=1.0 / D, op=ALU.mult)
                sdv.inc(inst, 1)
            t_st = sdv.n
            @SC
            def _(s, t_st=t_st):
                s.wait_ge(sdv.h, t_st)
                inst = s.activation(out=STAT[:, 4:5], in_=STAT[:, 2:3], func=AF.Square,
                                    bias=f32(0.0), scale=f32(1.0))
                sac.inc(inst, 1)
            t_msq = sac.n
            @VE
            def _(v, t_msq=t_msq):
                v.wait_ge(sac.h, t_msq)
                inst = v.tensor_tensor(out=STAT[:, 4:5], in0=STAT[:, 3:4], in1=STAT[:, 4:5],
                                       op=ALU.subtract)
                sdv.inc(inst, 1)
            t_var = sdv.n
            @SC
            def _(s, t_var=t_var):
                s.wait_ge(sdv.h, t_var)
                inst = s.activation(out=STAT[:, 6:7], in_=STAT[:, 4:5], func=AF.Sqrt,
                                    bias=f32(1e-5), scale=f32(1.0))
                sac.inc(inst, 1)
            t_sqv = sac.n
            @VE
            def _(v, t_sqv=t_sqv):
                v.wait_ge(sac.h, t_sqv)
                inst = v.reciprocal(STAT[:, 5:6], STAT[:, 6:7])
                sdv.inc(inst, 1)
                v.wait_ge(sdv.h, sdv.n)
                for h, n in apply_waits:
                    v.wait_ge(h, n)
                inst = ln_apply(v, src)
                sdv.inc(inst, 1)
            return sdv.n

        xw_t = xw.rearrange("(c p) d -> c p d", p=128)
        xw_free = [0, 0]
        xt1_tok = [0] * NWC
        tr_done = [0] * NWC

        for wc in range(min(NWC, p1n)):
            buf = wc % 2
            @SY
            def _(q, wc=wc, buf=buf):
                if wc >= 2:
                    q.wait_ge(sdv.h, xw_free[buf])
                q.dma_start(out=XW[buf][:], in_=xw_t[wc]).then_inc(qs.h, 16)
            qs.n += 16
            t_x = qs.n

            @SC
            def _(s, wc=wc, buf=buf, t_x=t_x):
                s.wait_ge(qs.h, t_x)
                if wc >= 1:
                    s.wait_ge(sdv.h, xt1_tok[wc - 1])
                inst = s.activation(out=SQS[:], in_=XW[buf][:], func=AF.Square,
                                    bias=f32(0.0), scale=f32(1.0))
                sac.inc(inst, 1)
            t_sq = sac.n

            t_xln = emit_ln_chain(
                stats_waits=[(qs.h, t_x), (sac.h, t_sq)],
                apply_waits=([(spe.h, tr_done[wc - 1])] if wc >= 1 else []),
                src=XW[buf])
            xw_free[buf] = t_xln

            for ck in range(NDC if p1tr else 0):
                @TE
                def _(t, wc=wc, ck=ck, t_xln=t_xln):
                    t.wait_ge(sdv.h, t_xln)
                    if ck >= 2:
                        t.wait_ge(sdv.h, sdv.n - 1)
                    inst = t.transpose(PBT[ck % 2], XLN[:, ck * 128:(ck + 1) * 128], IDENTB[:])
                    spe.inc(inst, 1)
                t_tr = spe.n
                @VE
                def _(v, wc=wc, ck=ck, t_tr=t_tr):
                    v.wait_ge(spe.h, t_tr)
                    inst = v.tensor_copy(XT1[:, ck * WIN + wc * 128: ck * WIN + wc * 128 + 128],
                                         PBT[ck % 2])
                    sdv.inc(inst, 1)
            xt1_tok[wc] = sdv.n
            tr_done[wc] = spe.n
            if debug and wc == 1:
                @SC
                def _(s, buf=buf, t_xln=t_xln):
                    s.wait_ge(sdv.h, t_xln)
                    s.dma_start(out=dbg["xln2c"], in_=XLN[:]).then_inc(sst.h, 16)
                    s.dma_start(out=dbg["xw2c"], in_=XW[buf][:]).then_inc(sst.h, 16)
                    s.dma_start(out=dbg["stat2c"], in_=STAT[:]).then_inc(sst.h, 16)
                sst.n += 48
        t_allxt1 = xt1_tok[NWC - 1]

        if debug:
            @SC
            def _(s):
                s.wait_ge(sdv.h, t_allxt1)
                s.dma_start(out=dbg["xt1"], in_=XT1[:]).then_inc(sst.h, 16)
            sst.n += 16

        emit_on[0] = phases >= 2
        t_pss_free = sdv.n
        for wc in range(NWC):
            @TE
            def _(t, wc=wc, t_pss_free=t_pss_free):
                t.wait_ge(sdv.h, xt1_tok[wc])
                t.wait_ge(qg.h, t_wf)
                t.wait_ge(sdv.h, t_pss_free)
                for ck in range(NDC):
                    t.matmul(PSS[:, 0:H], XT1[:, ck * WIN + wc * 128: ck * WIN + wc * 128 + 128],
                             WFT[:, ck * H:(ck + 1) * H], start=(ck == 0), stop=False)
                inst = t.matmul(PSS[:, 0:H], ONESRB[0:1, :], BFRB[0:1, :], start=False, stop=True)
                spe.inc(inst, 1)
            t_lf = spe.n

            @VE
            def _(v, wc=wc, t_lf=t_lf):
                v.wait_ge(spe.h, t_lf)
                inst = v.tensor_single_scalar(out=LOGF[:, wc * H:(wc + 1) * H], in_=PSS[:, 0:H],
                                              scalar=0.0, op=ALU.min)
                sdv.inc(inst, 1)

            @SC
            def _(s, wc=wc, t_lf=t_lf):
                s.wait_ge(spe.h, t_lf)
                s.activation(out=LTMP[:], in_=PSS[:, 0:H], func=AF.Exp, bias=f32(0.0), scale=f32(1.0))
                inst = s.activation(out=LTM2[:], in_=PSS[:, 0:H], func=AF.Exp, bias=f32(0.0), scale=f32(-1.0))
                sac.inc(inst, 1)
            t_ee = sac.n

            @VE
            def _(v, wc=wc, t_ee=t_ee):
                v.wait_ge(sac.h, t_ee)
                inst = v.tensor_tensor(out=UALL[:, wc * H:(wc + 1) * H],
                                       in0=LTMP[:], in1=LTM2[:], op=ALU.min)
                sdv.inc(inst, 1)
            t_pss_free = sdv.n

        t_u_all = sdv.n
        @VE
        def _(v):
            v.wait_ge(sdv.h, t_u_all)
            v.tensor_scalar(out=HORN[:], in0=UALL[:], scalar1=LN1P_C[8],
                            scalar2=LN1P_C[7], op0=ALU.mult, op1=ALU.add)
            for k in range(6, -1, -1):
                v.tensor_tensor(out=HORN[:], in0=HORN[:], in1=UALL[:], op=ALU.mult)
                v.tensor_single_scalar(out=HORN[:], in_=HORN[:], scalar=LN1P_C[k], op=ALU.add)
            v.tensor_tensor(out=HORN[:], in0=HORN[:], in1=UALL[:], op=ALU.mult)
            inst = v.tensor_tensor(out=LOGF[:], in0=LOGF[:], in1=HORN[:], op=ALU.subtract)
            sdv.inc(inst, 1)
        t_logf = sdv.n

        t_pssr = t_logf
        for wc in range(NWC):
            @TE
            def _(t, wc=wc, t_pssr=t_pssr):
                t.wait_ge(sdv.h, t_logf)
                t.wait_ge(sdv.h, t_pssr)
                for j in range(wc + 1):
                    lhsT = ONESF if j < wc else TRIF
                    inst = t.matmul(PSS[:, 0:H], lhsT[:],
                                    LOGF[:, j * H:(j + 1) * H], start=(j == 0), stop=(j == wc))
                spe.inc(inst, 1)
            t_c = spe.n

            @VE
            def _(v, wc=wc, t_c=t_c):
                v.wait_ge(spe.h, t_c)
                v.wait_ge(spo.h, t_masks)
                v.wait_ge(qs.h, t_km)
                v.tensor_copy(CCOL[:, wc * H:(wc + 1) * H], PSS[:, 0:H])
                inst = v.tensor_scalar(out=NEGC[:, wc * H:(wc + 1) * H],
                                       in0=PSS[:, 0:H],
                                       scalar1=KM[:, wc:wc + 1], scalar2=-1.0,
                                       op0=ALU.subtract, op1=ALU.mult)
                sdv.inc(inst, 1)
            t_pssr = sdv.n
        t_negc = sdv.n

        for qc in range(NQC):
            @TE
            def _(t, qc=qc):
                t.wait_ge(sdv.h, t_negc)
                if qc >= 2:
                    t.wait_ge(sdv.h, sdv.n - 1)
                inst = t.transpose(PFT[qc % 2][0:32, :], CCOL[:, (3 + qc) * H:(4 + qc) * H],
                                   IDENTF[:])
                spe.inc(inst, 1)
            t_tr = spe.n
            @VE
            def _(v, qc=qc, t_tr=t_tr):
                v.wait_ge(spe.h, t_tr)
                inst = v.tensor_copy(CROWB[0:32, qc * 128:(qc + 1) * 128], PFT[qc % 2][0:32, :])
                sdv.inc(inst, 1)
        t_crow = sdv.n

        emit_on[0] = phases >= 3
        KTIL = [(0, 512), (512, 512), (1024, 384)]
        wpt_free = [0, 0]

        @VE
        def _(v):
            for wc in range(NWC):
                for lh in range(8):
                    v.memset(Vg[:, wc * 520 + lh * 65 + 64: wc * 520 + lh * 65 + 65], 1.0)
            inst = v.memset(STAT[:, 15:16], 0.0)
            sdv.inc(inst, 1)
        t_vones = sdv.n

        psA_free = [0, 0, 0]
        oscr_qg = [0, 0]
        prev_exp = [0]
        prev_av = [0]
        ps01_free = [0, 0]
        for gi in range(4):
            @GP
            def _(g, gi=gi):
                g.wait_ge(sdv.h, wpt_free[0])
                inst = g.dma_start(
                    out=WPT[0].rearrange("p (c n) -> p c n", c=NDC),
                    in_=wqT.rearrange("(c p) n -> p c n", p=128)[:, :, gi * 512:(gi + 1) * 512])
                inst.then_inc(qg.h, 16)
            qg.n += 16
            tok_q = qg.n
            for k in range(8):
                pr, ti = k // 2, k % 2
                ps = PS4 if k % 2 == 0 else PS5
                @TE
                def _(t, gi=gi, pr=pr, ti=ti, ps=ps, k=k, tok_q=tok_q, sg=sdv.n):
                    t.wait_ge(qg.h, tok_q)
                    t.wait_ge(sdv.h, t_allxt1)
                    if k >= 2:
                        t.wait_ge(sdv.h, sdv.n - 1)
                    else:
                        t.wait_ge(sdv.h, sg)
                    for ck in range(NDC):
                        inst = t.matmul(
                            ps[:], WPT[0][:, ck * 512 + pr * 128: ck * 512 + (pr + 1) * 128],
                            XT1[:, ck * WIN + NPAD + ti * 512: ck * WIN + NPAD + (ti + 1) * 512],
                            start=(ck == 0), stop=(ck == NDC - 1))
                    spe.inc(inst, 1)
                t_mm = spe.n
                @VE
                def _(v, pr=pr, ti=ti, ps=ps, t_mm=t_mm):
                    v.wait_ge(spe.h, t_mm)
                    inst = v.tensor_single_scalar(
                        out=Qg[:, pr * QN + ti * 512: pr * QN + (ti + 1) * 512],
                        in_=ps[:], scalar=0.125, op=ALU.mult)
                    sdv.inc(inst, 1)
            wpt_free[0] = sdv.n

            @GP
            def _(g, gi=gi):
                g.wait_ge(sdv.h, wpt_free[1])
                if gi == 0:
                    g.wait_ge(sdv.h, t_allxt1)
                inst = g.dma_start(
                    out=WPT[1].rearrange("p (c n) -> p c n", c=NDC),
                    in_=wkT.rearrange("(c p) n -> p c n", p=128)[:, :, gi * 512:(gi + 1) * 512])
                inst.then_inc(qg.h, 16)
            qg.n += 16
            tok_k = qg.n
            k = 0
            for pr in range(4):
                for (w0, wl) in KTIL:
                    ps = PS4 if k % 2 == 0 else PS5
                    @TE
                    def _(t, pr=pr, w0=w0, wl=wl, ps=ps, k=k, tok_k=tok_k, sg=sdv.n):
                        t.wait_ge(qg.h, tok_k)
                        if k >= 2:
                            t.wait_ge(sdv.h, sdv.n - 1)
                        else:
                            t.wait_ge(sdv.h, sg)
                        for ck in range(NDC):
                            inst = t.matmul(
                                ps[:, 0:wl],
                                WPT[1][:, ck * 512 + pr * 128: ck * 512 + (pr + 1) * 128],
                                XT1[:, ck * WIN + w0: ck * WIN + w0 + wl],
                                start=(ck == 0), stop=(ck == NDC - 1))
                        spe.inc(inst, 1)
                    t_mm = spe.n
                    @VE
                    def _(v, pr=pr, w0=w0, wl=wl, ps=ps, t_mm=t_mm):
                        v.wait_ge(spe.h, t_mm)
                        inst = v.tensor_copy(Kg[:, pr * WIN + w0: pr * WIN + w0 + wl], ps[:, 0:wl])
                        sdv.inc(inst, 1)
                    k += 1
            wpt_free[1] = sdv.n

            @GP
            def _(g, gi=gi):
                g.wait_ge(sdv.h, wpt_free[0])
                inst = g.dma_start(
                    out=WPT[0].rearrange("p (c n) -> p c n", c=NDC),
                    in_=wvT.rearrange("(c p) n -> p c n", p=128)[:, :, gi * 512:(gi + 1) * 512])
                inst.then_inc(qg.h, 16)
            qg.n += 16
            tok_v = qg.n
            for wc in range(NWC):
                ps = PS4 if wc % 2 == 0 else PS5
                @TE
                def _(t, wc=wc, ps=ps, tok_v=tok_v, sg=sdv.n):
                    t.wait_ge(qg.h, tok_v)
                    t.wait_ge(sdv.h, t_vones)
                    if wc >= 2:
                        t.wait_ge(sdv.h, sdv.n - 1)
                    else:
                        t.wait_ge(sdv.h, sg)
                    for ck in range(NDC):
                        inst = t.matmul(
                            ps[:], XT1[:, ck * WIN + wc * 128: ck * WIN + (wc + 1) * 128],
                            WPT[0][:, ck * 512:(ck + 1) * 512],
                            start=(ck == 0), stop=(ck == NDC - 1))
                    spe.inc(inst, 1)
                t_mm = spe.n
                @VE
                def _(v, wc=wc, ps=ps, t_mm=t_mm):
                    v.wait_ge(spe.h, t_mm)
                    inst = v.tensor_copy(
                        Vg.rearrange("p (c f) -> p c f", c=NWC)[:, wc].rearrange(
                            "p (h d) -> p h d", h=8)[:, :, 0:64],
                        ps.ap().rearrange("p (h d) -> p h d", h=8))
                    sdv.inc(inst, 1)
            wpt_free[0] = sdv.n
            t_kv = sdv.n

            for lh in range(8):
                gh = gi * 8 + lh
                pr, side = lh // 2, lh % 2
                r0 = side * 64
                for ti in range(2):
                    jcs = list(range(4 * ti, 4 * ti + 7))
                    psAb = [PS2, PS3, PS6][(lh * 2 + ti) % 3]
                    psAi = (lh * 2 + ti) % 3
                    psA = psAb.ap()[0:65, :]
                    for idx, jc in enumerate(jcs):
                        sb = PS0 if idx % 2 == 0 else PS1
                        @TE
                        def _(t, gh=gh, pr=pr, r0=r0, ti=ti, jc=jc, sb=sb, idx=idx,
                              pf0=ps01_free[0], pf1=ps01_free[1]):
                            t.wait_ge(sdv.h, t_kv)
                            t.wait_ge(sdv.h, t_crow)
                            t.wait_ge(sdv.h, (pf0 if idx % 2 == 0 else pf1))
                            t.matmul(
                                sb[:],
                                Kg[r0:r0 + 64, pr * WIN + jc * 128: pr * WIN + (jc + 1) * 128],
                                Qg[r0:r0 + 64, pr * QN + ti * 512: pr * QN + (ti + 1) * 512],
                                start=True, stop=False)
                            inst = t.matmul(
                                sb[:], SELS[:, gh * 128:(gh + 1) * 128],
                                CROWB[0:32, ti * 512:(ti + 1) * 512],
                                start=False, stop=True)
                            spe.inc(inst, 1)
                        t_s = spe.n
                        @VE
                        def _(v, sb=sb, idx=idx, gh=gh, jc=jc, t_s=t_s, pe=prev_exp[0]):
                            v.wait_ge(spe.h, t_s)
                            if idx == 0:
                                v.wait_ge(sac.h, pe)
                            inst = v.tensor_single_scalar(
                                out=SCR[:, idx * 512:(idx + 1) * 512], in_=sb[:],
                                scalar=NEGC[:, jc * H + gh: jc * H + gh + 1], op=ALU.add)
                            sdv.inc(inst, 1)
                        ps01_free[idx % 2] = sdv.n
                    t_evac = sdv.n
                    @SC
                    def _(s, t_evac=t_evac, pav=prev_av[0]):
                        s.wait_ge(sdv.h, t_evac)
                        s.wait_ge(spe.h, pav)
                        inst = s.activation(out=EXPO[:], in_=SCR[:], func=AF.Exp,
                                            bias=f32(0.0), scale=f32(1.0))
                        sac.inc(inst, 1)
                    t_exp = sac.n
                    prev_exp[0] = t_exp
                    sel_tok = {}
                    for idx, jc in enumerate(jcs):
                        mbase = NPAD + 512 * ti - 128 * jc
                        if mbase < 127:
                            @GP
                            def _(g, idx=idx, mbase=mbase, t_exp=t_exp):
                                g.wait_ge(sac.h, t_exp)
                                inst = g.affine_select(
                                    out=EXPO[:, idx * 512:(idx + 1) * 512],
                                    in_=EXPO[:, idx * 512:(idx + 1) * 512],
                                    pattern=[[1, 512]], compare_op=ALU.is_ge, fill=0.0,
                                    base=mbase, channel_multiplier=-1)
                                spo.inc(inst, 1)
                            sel_tok[idx] = spo.n
                    for idx, jc in enumerate(jcs):
                        @TE
                        def _(t, jc=jc, idx=idx, psA=psA, lh=lh, t_exp=t_exp,
                              st=sel_tok.get(idx), psAi=psAi, n=len(jcs)):
                            t.wait_ge(sac.h, t_exp)
                            if st is not None:
                                t.wait_ge(spo.h, st)
                            if idx == 0:
                                t.wait_ge(sdv.h, psA_free[psAi])
                            inst = t.matmul(psA,
                                            Vg[:, jc * 520 + lh * 65: jc * 520 + lh * 65 + 65],
                                            EXPO[:, idx * 512:(idx + 1) * 512],
                                            start=(idx == 0), stop=(idx == n - 1))
                            spe.inc(inst, 1)
                    t_av = spe.n
                    prev_av[0] = t_av
                    @VE
                    def _(v, psAb=psAb, t_av=t_av):
                        v.wait_ge(spe.h, t_av)
                        inst = v.reciprocal(RRB[64:65, :], psAb.ap()[64:65, :])
                        sdv.inc(inst, 1)
                    t_rr = sdv.n
                    @TE
                    def _(t, t_rr=t_rr):
                        t.wait_ge(sdv.h, t_rr)
                        inst = t.matmul(PS4.ap()[0:64, :], ONESR64[64:65, 0:64], RRB[64:65, :],
                                        start=True, stop=True)
                        spe.inc(inst, 1)
                    t_rb = spe.n
                    @VE
                    def _(v, t_rb=t_rb):
                        v.wait_ge(spe.h, t_rb)
                        inst = v.tensor_copy(RBC[:], PS4.ap()[0:64, :])
                        sdv.inc(inst, 1)
                    if side == 0:
                        @VE
                        def _(v, psAb=psAb, gh=gh, ti=ti):
                            inst = v.tensor_tensor(
                                out=ATT[0:64, (gh // 2) * QN + ti * 512: (gh // 2) * QN + (ti + 1) * 512],
                                in0=psAb.ap()[0:64, :], in1=RBC[:], op=ALU.mult)
                            sdv.inc(inst, 1)
                        psA_free[psAi] = sdv.n
                    else:
                        oi = ti
                        @VE
                        def _(v, psAb=psAb, oi=oi, t_dma=oscr_qg[oi]):
                            v.wait_ge(qg.h, t_dma)
                            inst = v.tensor_tensor(out=OSC[oi][:], in0=psAb.ap()[0:64, :],
                                                   in1=RBC[:], op=ALU.mult)
                            sdv.inc(inst, 1)
                        psA_free[psAi] = sdv.n
                        t_os = sdv.n
                        @GP
                        def _(g, oi=oi, gh=gh, ti=ti, t_os=t_os):
                            g.wait_ge(sdv.h, t_os)
                            g.dma_start(
                                out=ATT[64:128, (gh // 2) * QN + ti * 512: (gh // 2) * QN + (ti + 1) * 512],
                                in_=OSC[oi][:]).then_inc(qg.h, 16)
                        qg.n += 16
                        oscr_qg[oi] = qg.n
        t_att_v = sdv.n
        t_att_g = qg.n

        if debug:
            @SC
            def _(s):
                s.wait_ge(sdv.h, t_att_v)
                s.wait_ge(qg.h, t_att_g)
                s.dma_start(out=dbg["logf"], in_=LOGF[:]).then_inc(sst.h, 16)
                s.dma_start(out=dbg["ccol"], in_=CCOL[:]).then_inc(sst.h, 16)
                s.dma_start(out=dbg["att"], in_=ATT[:]).then_inc(sst.h, 16)
            sst.n += 48

        emit_on[0] = phases >= 4
        wot_free = [0, 0]
        for dc in range(NDC if p3o else 0):
            slot = dc % 2
            @GP
            def _(g, dc=dc, slot=slot):
                g.wait_ge(sdv.h, wot_free[slot])
                inst = g.dma_start(
                    out=WOTB[slot].rearrange("p (c n) -> p c n", c=NDC),
                    in_=woT.rearrange("(c p) n -> p c n", p=128)[:, :, dc * 128:(dc + 1) * 128])
                inst.then_inc(qg.h, 16)
            qg.n += 16
            tok_w = qg.n
            for ti in range(2):
                ps = PS0 if ti == 0 else PS1
                @TE
                def _(t, dc=dc, ti=ti, ps=ps, slot=slot, tok_w=tok_w, ag=sac.n):
                    t.wait_ge(qg.h, tok_w)
                    t.wait_ge(sdv.h, t_att_v)
                    t.wait_ge(qg.h, t_att_g)
                    t.wait_ge(sac.h, ag)
                    if dc + ti > 0:
                        t.wait_ge(sdv.h, sdv.n - 1)
                    for ck in range(NDC):
                        inst = t.matmul(ps[:],
                                        WOTB[slot][:, ck * 128:(ck + 1) * 128],
                                        ATT[:, ck * QN + ti * 512: ck * QN + ti * 512 + 512],
                                        start=(ck == 0), stop=(ck == NDC - 1))
                    spe.inc(inst, 1)
                t_o = spe.n
                @VE
                def _(v, dc=dc, ti=ti, ps=ps, t_o=t_o):
                    v.wait_ge(spe.h, t_o)
                    inst = v.tensor_copy(OTB[:, dc * QN + ti * 512: dc * QN + (ti + 1) * 512], ps[:])
                    sdv.inc(inst, 1)
            wot_free[slot] = sdv.n
        t_otb = sdv.n

        xt2_tok = [0] * NQC
        t_prev_x2 = 0
        out2i_qg = 0
        for qc in range(min(NQC, p3n)):
            wc = qc + 3
            buf = qc % 2
            @SY
            def _(q, qc=qc, wc=wc, buf=buf):
                q.wait_ge(sdv.h, xw_free[buf])
                if qc < 2:
                    q.wait_ge(sdv.h, t_att_v)
                q.dma_start(out=XW[buf][:], in_=xw_t[wc]).then_inc(qs.h, 16)
            qs.n += 16
            t_x = qs.n

            for ck in range(NDC):
                @TE
                def _(t, qc=qc, ck=ck, t_prev_x2=t_prev_x2):
                    t.wait_ge(sdv.h, t_otb)
                    t.wait_ge(sdv.h, t_prev_x2)
                    if ck >= 2:
                        t.wait_ge(sdv.h, sdv.n - 1)
                    inst = t.transpose(PBT[ck % 2],
                                       OTB[:, ck * QN + qc * 128: ck * QN + qc * 128 + 128],
                                       IDENTB[:])
                    spe.inc(inst, 1)
                t_tr = spe.n
                @VE
                def _(v, qc=qc, ck=ck, buf=buf, t_tr=t_tr, t_x=t_x):
                    v.wait_ge(spe.h, t_tr)
                    v.wait_ge(qs.h, t_x)
                    inst = v.tensor_tensor(
                        out=X2S[:, ck * 128:(ck + 1) * 128],
                        in0=XW[buf][:, ck * 128:(ck + 1) * 128],
                        in1=PBT[ck % 2],
                        op=ALU.add)
                    sdv.inc(inst, 1)
            xw_free[buf] = sdv.n
            t_x2 = sdv.n

            if debug:
                @SC
                def _(s, qc=qc, t_x2=t_x2):
                    s.wait_ge(sdv.h, t_x2)
                    s.dma_start(out=dbg["x2"].rearrange("(c p) d -> c p d", p=128)[qc],
                                in_=X2S[:]).then_inc(sst.h, 16)
                sst.n += 16

            @SC
            def _(s, qc=qc, t_x2=t_x2):
                s.wait_ge(sdv.h, t_x2)
                inst = s.activation(out=SQS[:], in_=X2S[:], func=AF.Square, bias=f32(0.0), scale=f32(1.0))
                sac.inc(inst, 1)
            t_sq = sac.n
            t_xln2 = emit_ln_chain(
                stats_waits=[(sac.h, t_sq)],
                apply_waits=([(spe.h, spe.n)] if qc >= 1 else []),
                src=X2S)

            for gch in range(NDC):
                ck, nib = gch // 2, gch % 2
                @TE
                def _(t, qc=qc, gch=gch, ck=ck, nib=nib, t_xln2=t_xln2):
                    t.wait_ge(sdv.h, t_xln2)
                    if gch >= 2:
                        t.wait_ge(sdv.h, sdv.n - 1)
                    inst = t.transpose(PBT[gch % 2],
                                       XLN[:, 256 * ck + nib: 256 * ck + nib + 255: 2],
                                       IDENTB[:])
                    spe.inc(inst, 1)
                t_tr = spe.n
                @VE
                def _(v, qc=qc, gch=gch, t_tr=t_tr):
                    v.wait_ge(spe.h, t_tr)
                    inst = v.tensor_copy(XT2[:, gch * QN + qc * 128: gch * QN + qc * 128 + 128],
                                         PBT[gch % 2])
                    sdv.inc(inst, 1)
            xt2_tok[qc] = sdv.n

            for ck in range(NDC):
                @TE
                def _(t, qc=qc, ck=ck):
                    t.wait_ge(sdv.h, xt2_tok[qc])
                    if ck >= 2:
                        t.wait_ge(sdv.h, sdv.n - 1)
                    inst = t.transpose(PFT[ck % 2], X2S[:, ck * 128:(ck + 1) * 128], IDENTF[:])
                    spe.inc(inst, 1)
                t_tr = spe.n
                @VE
                def _(v, ck=ck, t_tr=t_tr, t_o2p=out2i_qg):
                    v.wait_ge(spe.h, t_tr)
                    if ck == 0:
                        v.wait_ge(qg.h, t_o2p)
                    inst = v.tensor_copy(OUTS[:, ck * 128:(ck + 1) * 128], PFT[ck % 2])
                    sdv.inc(inst, 1)
            t_prev_x2 = sdv.n
            t_o2i = sdv.n
            @GP
            def _(g, qc=qc, t_o2i=t_o2i):
                g.wait_ge(sdv.h, t_o2i)
                inst = g.dma_start(
                    out=out2d.rearrange("(c p) w -> p c w", p=128)[:, :, qc * 128:(qc + 1) * 128],
                    in_=OUTS.rearrange("p (c w) -> p c w", c=NDC))
                inst.then_inc(qg.h, 16)
            qg.n += 16
            out2i_qg = qg.n
        t_out2init = qg.n
        t_xt2_all = xt2_tok[NQC - 1]

        if debug:
            @SC
            def _(s):
                s.wait_ge(sdv.h, t_xt2_all)
                s.dma_start(out=dbg["xt2"], in_=XT2[:]).then_inc(sst.h, 16)
            sst.n += 16

        emit_on[0] = phases >= 5
        pk_free = [0, 0, 0, 0]
        slotc = [0]
        dec_par = [0]
        ffn_pre = [True]

        def decode_chunk(pk_ap, sc_ap, dst_both, wd):
            A, B, AB = slice(0, wd), slice(wd, 2 * wd), slice(0, 2 * wd)
            par = dec_par[0] % 2
            dec_par[0] += 1
            DF = DF0 if par == 0 else DF0B
            @VE
            def _(v, DF=DF):
                v.tensor_scalar(out=DF[:, A], in0=pk_ap, scalar1=7, scalar2=6,
                                op0=ALU.bitwise_and, op1=ALU.logical_shift_left)
                v.tensor_scalar(out=DF[:, B], in0=pk_ap, scalar1=0x70, scalar2=2,
                                op0=ALU.bitwise_and, op1=ALU.logical_shift_left)
                v.tensor_scalar(out=DSGN[:, A], in0=pk_ap, scalar1=8, scalar2=12,
                                op0=ALU.bitwise_and, op1=ALU.logical_shift_left)
                v.tensor_scalar(out=DSGN[:, B], in0=pk_ap, scalar1=128, scalar2=8,
                                op0=ALU.bitwise_and, op1=ALU.logical_shift_left)
                v.tensor_single_scalar(out=DF[:, AB], in_=DF[:, AB], scalar=0x3F00, op=ALU.add)
                v.tensor_scalar(out=DF1.bitcast(BF16)[:, AB], in0=DF.bitcast(BF16)[:, AB],
                                scalar1=2.0, scalar2=-1.0, op0=ALU.mult, op1=ALU.add)
                v.tensor_tensor(out=DF.bitcast(BF16)[:, AB], in0=DF.bitcast(BF16)[:, AB],
                                in1=DF1.bitcast(BF16)[:, AB], op=ALU.min)
                v.tensor_tensor(out=DF[:, AB], in0=DF[:, AB], in1=DSGN[:, AB],
                                op=ALU.bitwise_or)
                v.tensor_tensor(out=dst_both[:, 0:wd], in0=DF.bitcast(BF16)[:, A],
                                in1=sc_ap, op=ALU.mult)
                inst = v.tensor_tensor(out=dst_both[:, wd:2 * wd], in0=DF.bitcast(BF16)[:, B],
                                       in1=sc_ap, op=ALU.mult)
                sdv.inc(inst, 1)
            return sdv.n

        gw_free = [0, 0]
        uw_free = [0, 0]
        gw_tok = [0, 0]
        uw_tok = [0, 0]
        inter_last = 0
        dwn_free = [0, 0]

        def load_decode_window(which, w, buf):
            src_pk = gpk if which == "g" else upk
            src_sc = gsc if which == "g" else usc
            dstT = (GW if which == "g" else UW)[buf]
            freetok = (gw_free if which == "g" else uw_free)[buf]
            last = 0
            for ck in range(8):
                slot = slotc[0] % 4
                slotc[0] += 1
                @SY
                def _(q, w=w, ck=ck, slot=slot, freetok=freetok, src_pk=src_pk, src_sc=src_sc):
                    q.wait_ge(sdv.h, pk_free[slot])
                    q.wait_ge(spe.h, freetok)
                    q.dma_start(out=PKB[slot][:],
                                in_=src_pk[w * 512:(w + 1) * 512, ck * 128:(ck + 1) * 128],
                                transpose=True).then_inc(qs.h, 16)
                    q.dma_start(out=SCB[slot][:],
                                in_=src_sc[ck * 128:(ck + 1) * 128, w * 512:(w + 1) * 512]
                                ).then_inc(qs.h, 16)
                qs.n += 32
                t_pk = qs.n
                @VE
                def _(v, t_pk=t_pk):
                    v.wait_ge(qs.h, t_pk)
                last = decode_chunk(PKB[slot][:], SCB[slot][:],
                                    dstT[:, (2 * ck) * 512:(2 * ck + 2) * 512], 512)
                pk_free[slot] = last
            if which == "g":
                gw_tok[buf] = last
            else:
                uw_tok[buf] = last

        psGU_free = [0, 0, 0, 0]
        for s_ in range(NSUP):
            for wv in range(4):
                w = s_ * 4 + wv
                buf = w % 2
                load_decode_window("g", w, buf)
                load_decode_window("u", w, buf)
                for nch in range(4):
                    n_in_sup = wv * 4 + nch
                    for ti in range(2):
                        par = (nch * 2 + ti) % 2
                        psG = PS0 if par == 0 else PS1
                        psU = PS2 if par == 0 else PS3
                        @TE
                        def _(t, w=w, buf=buf, nch=nch, ti=ti, psG=psG, psU=psU, par=par, sg=sdv.n):
                            t.wait_ge(sdv.h, t_xt2_all)
                            if w == 0 and nch == 0 and ti == 0:
                                t.wait_ge(sdv.h, sg)
                            t.wait_ge(sdv.h, gw_tok[buf])
                            t.wait_ge(sdv.h, uw_tok[buf])
                            t.wait_ge(sdv.h, psGU_free[par])
                            t.wait_ge(sdv.h, psGU_free[2 + par])
                            for g16 in range(16):
                                inst = t.matmul(
                                    psG[:], GW[buf][:, g16 * 512 + nch * 128: g16 * 512 + (nch + 1) * 128],
                                    XT2[:, g16 * QN + ti * 512: g16 * QN + (ti + 1) * 512],
                                    start=(g16 == 0), stop=(g16 == 15))
                            spe.inc(inst, 1)
                            for g16 in range(16):
                                inst = t.matmul(
                                    psU[:], UW[buf][:, g16 * 512 + nch * 128: g16 * 512 + (nch + 1) * 128],
                                    XT2[:, g16 * QN + ti * 512: g16 * QN + (ti + 1) * 512],
                                    start=(g16 == 0), stop=(g16 == 15))
                            spe.inc(inst, 1)
                        t_gu = spe.n
                        @VE
                        def _(v, psG=psG, t_gu=t_gu):
                            v.wait_ge(spe.h, t_gu)
                            v.tensor_copy(GF[:], psG[:])
                            v.tensor_tensor(out=SQT[:], in0=GF[:], in1=GF[:], op=ALU.mult)
                            v.tensor_scalar(out=YT[:], in0=SQT[:], scalar1=float(GELU_A), scalar2=float(GELU_B),
                                            op0=ALU.mult, op1=ALU.add)
                            inst = v.tensor_tensor(out=U2T[:], in0=YT[:], in1=GF[:], op=ALU.mult)
                            sdv.inc(inst, 1)
                        t_u2 = sdv.n
                        psGU_free[par] = t_u2
                        @SC
                        def _(s, t_u2=t_u2):
                            s.wait_ge(sdv.h, t_u2)
                            inst = s.activation(out=THT[:], in_=U2T[:], func=AF.Tanh,
                                                bias=f32(0.0), scale=f32(1.0))
                            sac.inc(inst, 1)
                        t_th = sac.n
                        @VE
                        def _(v, t_th=t_th):
                            v.wait_ge(sac.h, t_th)
                            inst = v.tensor_scalar(out=QT[:], in0=THT[:], scalar1=0.5,
                                                   scalar2=0.5, op0=ALU.mult, op1=ALU.add)
                            sdv.inc(inst, 1)
                        t_qt = sdv.n
                        @GP
                        def _(g, t_qt=t_qt):
                            g.wait_ge(sdv.h, t_qt)
                            inst = g.tensor_tensor(out=T1T[:], in0=QT[:], in1=GF[:], op=ALU.mult)
                            spo.inc(inst, 1)
                        t_t1 = spo.n
                        @VE
                        def _(v, psU=psU, n_in_sup=n_in_sup, ti=ti, t_t1=t_t1):
                            v.wait_ge(spo.h, t_t1)
                            inst = v.tensor_tensor(
                                out=INTER[:, n_in_sup * QN + ti * 512: n_in_sup * QN + (ti + 1) * 512],
                                in0=T1T[:], in1=psU[:], op=ALU.mult)
                            sdv.inc(inst, 1)
                        psGU_free[2 + par] = sdv.n
                        inter_last = sdv.n
                gw_free[buf] = spe.n
                uw_free[buf] = spe.n

            psD_free = [0, 0]
            if s_ == 0:
                dstg_qg = [0, 0]
            for hw in range(8):
                dbuf = hw % 2
                dw_last = 0
                for ck in range(8):
                    slot = slotc[0] % 4
                    slotc[0] += 1
                    @SY
                    def _(q, s_=s_, hw=hw, ck=ck, slot=slot, dbuf=dbuf):
                        q.wait_ge(sdv.h, pk_free[slot])
                        q.wait_ge(spe.h, dwn_free[dbuf])
                        q.dma_start(out=PKB[slot][0:128, 0:256],
                                    in_=dpk[hw * 256:(hw + 1) * 256,
                                            s_ * 1024 + ck * 128: s_ * 1024 + (ck + 1) * 128],
                                    transpose=True).then_inc(qs.h, 16)
                        q.dma_start(out=SCB[slot][0:128, 0:256],
                                    in_=dsc[s_ * 1024 + ck * 128: s_ * 1024 + (ck + 1) * 128,
                                            hw * 256:(hw + 1) * 256]).then_inc(qs.h, 16)
                    qs.n += 32
                    t_pk = qs.n
                    @VE
                    def _(v, t_pk=t_pk):
                        v.wait_ge(qs.h, t_pk)
                    dw_last = decode_chunk(PKB[slot][0:128, 0:256], SCB[slot][0:128, 0:256],
                                           DWNW[dbuf][:, (2 * ck) * 256:(2 * ck + 2) * 256], 256)
                    pk_free[slot] = dw_last
                    if debug and s_ == NSUP - 1 and hw == 7 and ck == 7:
                        @SC
                        def _(s, slot=slot, dw_last=dw_last):
                            s.wait_ge(sdv.h, dw_last)
                            s.dma_start(out=dbg["dpkb"], in_=PKB[slot][0:128, 0:256]).then_inc(sst.h, 16)
                            s.dma_start(out=dbg["dscb"], in_=SCB[slot][0:128, 0:256]).then_inc(sst.h, 16)
                            s.dma_start(out=dbg["dwnw"], in_=DWNW[1][:]).then_inc(sst.h, 16)
                        sst.n += 48
                for hc2 in range(2):
                    hcg = hw * 2 + hc2
                    for ti in range(2):
                        par = (hc2 * 2 + ti) % 2
                        ps = PS4 if par == 0 else PS5
                        @TE
                        def _(t, hc2=hc2, ti=ti, ps=ps, dbuf=dbuf, dw_last=dw_last, par=par):
                            t.wait_ge(sdv.h, dw_last)
                            t.wait_ge(sdv.h, inter_last)
                            t.wait_ge(sdv.h, psD_free[par])
                            for kc in range(16):
                                inst = t.matmul(
                                    ps[:],
                                    DWNW[dbuf][:, kc * 256 + hc2 * 128: kc * 256 + (hc2 + 1) * 128],
                                    INTER[:, kc * QN + ti * 512: kc * QN + (ti + 1) * 512],
                                    start=(kc == 0), stop=(kc == 15))
                            spe.inc(inst, 1)
                        t_dn = spe.n
                        dstg = DSTG[par]
                        @VE
                        def _(v, ps=ps, dstg=dstg, t_dn=t_dn, t_dma=dstg_qg[par]):
                            v.wait_ge(spe.h, t_dn)
                            v.wait_ge(qg.h, t_dma)
                            inst = v.tensor_copy(dstg[:], ps[:])
                            sdv.inc(inst, 1)
                        psD_free[par] = sdv.n
                        t_ds = sdv.n
                        @GP
                        def _(g, s_=s_, hcg=hcg, ti=ti, dstg=dstg, t_ds=t_ds):
                            g.wait_ge(sdv.h, t_ds)
                            if s_ == 0:
                                g.wait_ge(qg.h, t_out2init)
                            else:
                                g.wait_ge(qg.h, qg.n)
                            inst = g.dma_start(
                                out=out2d[hcg * 128:(hcg + 1) * 128, ti * 512:(ti + 1) * 512],
                                in_=dstg[:], accum_op=ALU.add)
                            inst.then_inc(qg.h, 16)
                        qg.n += 16
                        dstg_qg[par] = qg.n
                dwn_free[dbuf] = spe.n

        t_ffn_done = qg.n

        if debug:
            @SC
            def _(s):
                s.wait_ge(sdv.h, inter_last)
                s.dma_start(out=dbg["inter"], in_=INTER[:, 0:4 * QN]).then_inc(sst.h, 16)
                s.dma_start(out=dbg["gw"], in_=GW[0][:]).then_inc(sst.h, 16)
                s.wait_ge(qg.h, t_ffn_done)
                s.dma_start(out=dbg["out2d"], in_=out2d).then_inc(sst.h, 16)
                s.dma_start(out=dbg["dstg0"], in_=DSTG[0][:]).then_inc(sst.h, 16)
                s.dma_start(out=dbg["dstg1"], in_=DSTG[1][:]).then_inc(sst.h, 16)
            sst.n += 80

        emit_on[0] = phases >= 6
        t_prev_f = 0
        for qc in range(NQC):
            @SY
            def _(q, qc=qc, t_prev_f=t_prev_f):
                q.wait_ge(qg.h, t_ffn_done)
                q.wait_ge(sdv.h, t_prev_f)
                q.dma_start(out=X2S[:],
                            in_=out2d.rearrange("(c p) w -> p c w", p=128)[:, :, qc * 128:(qc + 1) * 128]
                            ).then_inc(qs.h, 16)
            qs.n += 16
            t_l = qs.n
            for ck in range(NDC):
                @TE
                def _(t, ck=ck, t_l=t_l, sg=sdv.n):
                    t.wait_ge(qs.h, t_l)
                    if ck >= 2:
                        t.wait_ge(sdv.h, sdv.n - 1)
                    else:
                        t.wait_ge(sdv.h, sg)
                    inst = t.transpose(PFT[ck % 2], X2S[:, ck * 128:(ck + 1) * 128], IDENTF[:])
                    spe.inc(inst, 1)
                t_tr = spe.n
                @VE
                def _(v, ck=ck, t_tr=t_tr, t_stp=sst.n):
                    v.wait_ge(spe.h, t_tr)
                    if ck == 0:
                        v.wait_ge(sst.h, t_stp)
                    inst = v.tensor_copy(OUTS[:, ck * 128:(ck + 1) * 128], PFT[ck % 2])
                    sdv.inc(inst, 1)
            t_of = sdv.n
            t_prev_f = t_of
            @SC
            def _(s, qc=qc, t_of=t_of):
                s.wait_ge(sdv.h, t_of)
                s.dma_start(out=out.rearrange("(c p) d -> c p d", p=128)[qc],
                            in_=OUTS[:]).then_inc(sst.h, 16)
            sst.n += 16

        @SC
        def _(s):
            s.wait_ge(sst.h, sst.n)

    return nc


def _perm_mlp():
    idx = np.arange(MLP).reshape(64, 256)
    return np.concatenate([idx[:, 0::2], idx[:, 1::2]], axis=1).reshape(-1)


def _prep_inputs(inp):
    x = np.ascontiguousarray(np.asarray(inp["L0_attn_x"], dtype=np.float32))
    Wq = np.asarray(inp["L0_attn_Wq"], dtype=np.float32)
    Wk = np.asarray(inp["L0_attn_Wk"], dtype=np.float32)
    Wv = np.asarray(inp["L0_attn_Wv"], dtype=np.float32)
    Wf = np.asarray(inp["L0_attn_Wf"], dtype=np.float32)
    bf_ = np.asarray(inp["L0_attn_bf"], dtype=np.float32)
    Wo = np.asarray(inp["L0_attn_Wo"], dtype=np.float32)
    gW = np.asarray(inp["L0_ffn_gate_W_packed"])
    gS = np.asarray(inp["L0_ffn_gate_scales"], dtype=np.float32)
    uW = np.asarray(inp["L0_ffn_up_W_packed"])
    uS = np.asarray(inp["L0_ffn_up_scales"], dtype=np.float32)
    dW = np.asarray(inp["L0_ffn_down_W_packed"])
    dS = np.asarray(inp["L0_ffn_down_scales"], dtype=np.float32)

    perm = _perm_mlp()
    g_u16 = np.ascontiguousarray(
        np.ascontiguousarray(gW.astype('<i4')).view(np.uint16).reshape(MLP, D // 2, 2)[:, :, 0][perm])
    u_u16 = np.ascontiguousarray(
        np.ascontiguousarray(uW.astype('<i4')).view(np.uint16).reshape(MLP, D // 2, 2)[:, :, 0][perm])
    d_u16 = np.ascontiguousarray(
        np.ascontiguousarray(dW.astype('<i4')).view(np.uint16).reshape(D, MLP // 2, 2)[:, :, 0])
    gscT = np.ascontiguousarray(np.repeat(gS[perm].T, 16, axis=0))
    uscT = np.ascontiguousarray(np.repeat(uS[perm].T, 16, axis=0))
    dscT = np.ascontiguousarray(np.repeat(dS.T, 16, axis=0))

    shared = dict(
        wqT=np.ascontiguousarray(Wq.T), wkT=np.ascontiguousarray(Wk.T),
        wvT=np.ascontiguousarray(Wv.T), woT=np.ascontiguousarray(Wo.T),
        wfT=np.ascontiguousarray(Wf.T), bfr=np.ascontiguousarray(bf_.reshape(1, H)),
        gpk=g_u16, upk=u_u16, dpk=d_u16, gsc=gscT, usc=uscT, dsc=dscT)

    in_maps = []
    for c in range(8):
        b, half = c // 2, c % 2
        if half == 0:
            xwc = np.concatenate([np.zeros((NPAD, D), np.float32), x[b, 0:QN]], axis=0)
            km = np.zeros((NWC, 128), np.float32)
            km.reshape(-1)[:NPAD] = NEG
        else:
            xwc = np.ascontiguousarray(x[b, T - WIN:T])
            km = np.zeros((NWC, 128), np.float32)
        m = dict(shared)
        m["xw"] = xwc
        m["km"] = np.ascontiguousarray(km.T)
        in_maps.append(m)
    return in_maps


def kernel(**inp):
    in_maps = _prep_inputs(inp)
    nc = _build(debug=False)
    res = run_bass_kernel_spmd(nc, in_maps, core_ids=list(range(8)))
    outp = np.empty((B, T, D), np.float32)
    for c in range(8):
        b, half = c // 2, c % 2
        outp[b, half * QN:(half + 1) * QN] = res.results[c]["out"]
    return outp
